# version 22
# baseline (speedup 1.0000x reference)
import numpy as np
import ml_dtypes
import concourse.bass as bass
import concourse.mybir as mybir
from concourse.bass_utils import run_bass_kernel_spmd
from contextlib import ExitStack

F32 = mybir.dt.float32
BF16 = mybir.dt.bfloat16
I32 = mybir.dt.int32
AF = mybir.ActivationFunctionType
ALU = mybir.AluOpType
AX = mybir.AxisListType
BF = ml_dtypes.bfloat16

D = 2048
L = 4096
HD = 128
NCH = 8
HPC = 4
NS_A, NS_B, NS_C = 3.4445, -4.775, 2.0315
BASE_LR_INV = float(np.log(np.expm1(0.01)))
FG = 5632
GU_ROWS = 2 * FG
DN_PAD = 768


class Prog:

    ENGSEC = {"pe": "tensor", "act": "scalar", "dve": "vector", "pool": "gpsimd", "sp": "sync"}
    DMAQ = {"dsp": "sp", "dact": "act", "dpool": "pool", "dcc": "pool"}

    NLANES = 16

    def __init__(self, nc, stack, same_engine_sync=True):
        self.nc = nc
        self.same_engine_sync = same_engine_sync
        self.sems = {}
        self.counts = {}
        self.watermark = {}
        self.writers = {}
        self.readers = {}
        self.qidx = {}
        self.queue = {e: [] for e in self.ENGSEC}
        for name in self.ENGSEC:
            self.sems[name] = stack.enter_context(nc.semaphore(f"sem_{name}"))
            self.counts[name] = 0
        for name in self.DMAQ:
            self.qidx[name] = 0
            if name == "dcc":
                self.sems[name] = stack.enter_context(nc.semaphore(f"sem_{name}"))
                self.counts[name] = 0
            else:
                for i in range(self.NLANES):
                    ln = f"{name}_{i}"
                    self.sems[ln] = stack.enter_context(nc.semaphore(f"sem_{ln}"))
                    self.counts[ln] = 0

    def _key(self, b):
        if isinstance(b, tuple):
            t, k = b
        else:
            t, k = b, None
        return getattr(t, "name", None) or str(t), k

    def _dep_edges(self, reads, writes):
        edges = {}
        for t, key in reads:
            w = self.writers.get(t, {})
            items = list(w.items()) if key is None else [(k, v) for k, v in w.items() if k == key or k is None]
            for _, (s, c) in items:
                edges[s] = max(edges.get(s, 0), c)
        for t, key in writes:
            w = self.writers.get(t, {})
            items = list(w.items()) if key is None else [(k, v) for k, v in w.items() if k == key or k is None]
            for _, (s, c) in items:
                edges[s] = max(edges.get(s, 0), c)
            r = self.readers.get(t, {})
            ritems = (
                [x for lst in r.values() for x in lst]
                if key is None
                else [x for k, lst in r.items() if k == key or k is None for x in lst]
            )
            for s, c in ritems:
                edges[s] = max(edges.get(s, 0), c)
        return edges

    def _waits_for(self, engine, edges):
        waits = []
        for s, c in sorted(edges.items()):
            if s == engine and (engine == "pe" or not self.same_engine_sync):
                continue
            wm = self.watermark.get((engine, s), 0)
            if c <= wm:
                continue
            waits.append((s, c))
            self.watermark[(engine, s)] = c
        return waits

    def _record(self, sem_name, reads, writes):
        cnt = self.counts[sem_name]
        for t, key in reads:
            self.readers.setdefault(t, {}).setdefault(key, []).append((sem_name, cnt))
        for t, key in writes:
            d = self.writers.setdefault(t, {})
            if key is None:
                d.clear()
                self.readers.pop(t, None)
            d[key] = (sem_name, cnt)
            r = self.readers.get(t, {})
            if key in r:
                r[key] = []

    def op(self, engine, fn, reads=(), writes=(), inc=True):
        reads = [self._key(b) for b in reads]
        writes = [self._key(b) for b in writes]
        waits = self._waits_for(engine, self._dep_edges(reads, writes))
        if inc:
            self.counts[engine] += 1
        self.queue[engine].append((waits, fn, engine if inc else None, 1))
        self._record(engine, reads, writes)

    def group(self, engine, items, reads=(), writes=()):
        greads = [self._key(b) for b in reads]
        gwrites = [self._key(b) for b in writes]
        n = len(items)
        for i, (fn, r, w) in enumerate(items):
            r = [self._key(b) for b in r]
            w = [self._key(b) for b in w]
            waits = self._waits_for(engine, self._dep_edges(r, w))
            last = i == n - 1
            if last:
                self.counts[engine] += 1
            self.queue[engine].append((waits, fn, engine if last else None, 1))
        self._record(engine, greads, gwrites)

    def dma(self, queue, out_ap, in_ap, reads=(), writes=(), transpose=False, **kw):
        engine = self.DMAQ[queue]
        lane = queue if queue == "dcc" else f"{queue}_{self.qidx[queue] % self.NLANES}"
        self.qidx[queue] = self.qidx.get(queue, 0) + 1
        reads = [self._key(b) for b in reads]
        writes = [self._key(b) for b in writes]
        waits = self._waits_for(engine, self._dep_edges(reads, writes))
        self.counts[lane] += 16
        if transpose:
            fn = lambda e, o=out_ap, i=in_ap: e.dma_start_transpose(o, i)
        else:
            fn = lambda e, o=out_ap, i=in_ap, kw=kw: e.dma_start(o, i, **kw)
        self.queue[engine].append((waits, fn, lane, 16))
        self._record(lane, reads, writes)

    def dma_fn(self, queue, fn, reads=(), writes=()):
        engine = self.DMAQ[queue]
        lane = queue if queue == "dcc" else f"{queue}_{self.qidx[queue] % self.NLANES}"
        self.qidx[queue] = self.qidx.get(queue, 0) + 1
        reads = [self._key(b) for b in reads]
        writes = [self._key(b) for b in writes]
        waits = self._waits_for(engine, self._dep_edges(reads, writes))
        self.counts[lane] += 16
        self.queue[engine].append((waits, fn, lane, 16))
        self._record(lane, reads, writes)

    def cc(self, kind, op, groups, ins_aps, outs_aps, reads=(), writes=()):
        reads = [self._key(b) for b in reads]
        writes = [self._key(b) for b in writes]
        waits = self._waits_for("pool", self._dep_edges(reads, writes))
        self.counts["dcc"] += 1
        fn = lambda e: e.collective_compute(kind, op, replica_groups=groups,
                                            ins=ins_aps, outs=outs_aps)
        self.queue["pool"].append((waits, fn, "dcc", 1))
        self._record("dcc", reads, writes)

    def barrier(self, exclude=("dcc",)):
        for e in self.ENGSEC:
            edges = {s: c for s, c in self.counts.items()
                     if c > 0 and s != e and s not in exclude}
            waits = self._waits_for(e, edges)
            if waits:
                self.queue[e].append((waits, None, None, 0))

    def finish_wait(self, engine):
        edges = {s: c for s, c in self.counts.items() if c > 0 and s != engine}
        waits = self._waits_for(engine, edges)
        self.queue[engine].append((waits, None, None, 0))

    def emit(self):
        with self.nc.Block() as blk:
            for engine, ops in self.queue.items():
                if not ops:
                    continue
                sec = getattr(blk, self.ENGSEC[engine])

                def make(ops=ops):
                    def body(e):
                        for waits, fn, inc_sem, inc_val in ops:
                            for s, c in waits:
                                e.wait_ge(self.sems[s], c)
                            if fn is not None:
                                inst = fn(e)
                                if inc_sem is not None:
                                    inst.then_inc(self.sems[inc_sem], inc_val)
                    return body

                sec(make())


G8 = [list(range(8))]
G44 = [[0, 1, 2, 3], [4, 5, 6, 7]]


def build_program(dumps=()):
    nc = bass.Bass("TRN2", target_bir_lowering=False)

    xin = nc.dram_tensor("x_b", [L, D], F32, kind="ExternalInput")
    xqin = nc.dram_tensor("x_q", [1024, D], F32, kind="ExternalInput")
    wqkv = nc.dram_tensor("wqkv", [D, 12 * HD], F32, kind="ExternalInput")
    wlrd = nc.dram_tensor("wlr", [D, 12], F32, kind="ExternalInput")
    blrd = nc.dram_tensor("blr", [1, 48], F32, kind="ExternalInput")
    w0d = nc.dram_tensor("w0", [HPC * HD, HD], F32, kind="ExternalInput")
    w1d = nc.dram_tensor("w1", [HPC * HD, HD], F32, kind="ExternalInput")
    w2d = nc.dram_tensor("w2", [HPC * HD, HD], F32, kind="ExternalInput")
    onwd = nc.dram_tensor("onw", [1, HD], F32, kind="ExternalInput")
    wprojd = nc.dram_tensor("wproj", [D, D], F32, kind="ExternalInput")
    wgud = nc.dram_tensor("wgu", [GU_ROWS, 256], F32, kind="ExternalInput")
    wdnd = nc.dram_tensor("wdn", [D, DN_PAD], F32, kind="ExternalInput")
    nwd = nc.dram_tensor("nw", [1, D], F32, kind="ExternalInput")
    OUT = nc.dram_tensor("out", [1024, D], F32, kind="ExternalOutput")

    ident_b = nc.inline_tensor(np.eye(HD, dtype=BF), name="ident_b")
    aI_c = nc.inline_tensor((NS_A * np.eye(HD)).astype(np.float32), name="aI_c")
    ones_col = nc.inline_tensor(np.ones((HD, 1), np.float32), name="ones_col")
    ones_rowc = nc.inline_tensor(np.ones((1, HD), np.float32), name="ones_rowc")
    invn = nc.inline_tensor(np.array([[1.0 / (GU_ROWS * D), 1.0 / (D * FG), 1.0, 1.0]],
                                     np.float32), name="invn")

    xn_dram = nc.dram_tensor("xn_dram", [L, D], BF16)
    qkv_dram = nc.dram_tensor("qkv_dram", [12 * HD, L], BF16)
    o_dram = nc.dram_tensor("o_dram", [HPC * HD, L], BF16)
    xch_a2a = nc.dram_tensor("xch_a2a", [512, L], BF16)
    xch_r = nc.dram_tensor("xch_r", [4 * 512, L], BF16)
    wqn_g = nc.dram_tensor("wqn_g", [GU_ROWS, 256], BF16)
    wqn_d = nc.dram_tensor("wqn_d", [D, DN_PAD], BF16)
    wTg_p = nc.dram_tensor("wTg_p", [256, GU_ROWS], BF16)
    wTd_p = nc.dram_tensor("wTd_p", [DN_PAD, D], BF16)
    wTg = nc.dram_tensor("wTg", [D, GU_ROWS], BF16)
    wTd = nc.dram_tensor("wTd", [8 * DN_PAD, D], BF16)
    red_in = nc.dram_tensor("red_in", [1, 4], F32)
    red_out = nc.dram_tensor("red_out", [1, 4], F32)
    xq_dram = nc.dram_tensor("xq_dram", [1024, D], BF16)
    xq2_dram = nc.dram_tensor("xq2_dram", [1024, FG], BF16)
    ss_dram = nc.dram_tensor("ss_dram", [8, L], BF16)
    hid_dram = nc.dram_tensor("hid_dram", [1024, FG], BF16)

    dump_outs = []
    scr = {"xn_dram": xn_dram, "qkv_dram": qkv_dram, "o_dram": o_dram,
           "xch_a2a": xch_a2a, "xch_r": xch_r, "wTg": wTg, "wTd": wTd,
           "xq_dram": xq_dram, "xq2_dram": xq2_dram, "red_out": red_out,
           "hid_dram": hid_dram, "ss_dram": ss_dram}
    for dn in dumps:
        src = scr[dn]
        dst = nc.dram_tensor("dump_" + dn, list(src.shape), src.dtype, kind="ExternalOutput")
        dump_outs.append((dn, dst, src))

    stack = ExitStack()
    with stack:
        p = Prog(nc, stack)
        sb = lambda name, shape, dt=F32: stack.enter_context(nc.sbuf_tensor(name, shape, dt))
        ps = lambda name, shape, dt=F32: stack.enter_context(nc.psum_tensor(name, shape, dt))

        psA = ps("psA", [128, 512])
        psB = ps("psB", [128, 512])
        psC = ps("psC", [128, 512])
        psD = ps("psD", [128, 512])
        psG = ps("psG", [128, 512])
        psH = ps("psH", [128, 512])
        psI = ps("psI", [128, 512])
        psT = ps("psT", [128, 128], BF16)
        psm = [psG, psH, psI]

        identb = sb("identb", [128, 128], BF16)
        aI = sb("aI", [128, 128])
        ones_c = sb("ones_c", [128, 1])
        ones_r = sb("ones_r", [1, 128])
        ones_rb = sb("ones_rb", [1, 128], BF16)
        onw_r = sb("onw_r", [1, 128])
        onw_c = sb("onw_c", [128, 1])
        blrB = sb("blrB", [128, 48])
        blr_sb = sb("blr_sb", [1, 48])
        invn_sb = sb("invn_sb", [1, 4])
        wf = sb("wf", [128, 12 * 128])
        wbf = sb("wbf", [128, 12 * 128], BF16)
        w1T = sb("w1T", [128, 4 * 128], BF16)
        wn_row = sb("wn_row", [1, 12 * 128])
        wsq = sb("wsq", [128, 12 * 128])
        ssw_row = sb("ssw_row", [1, 12 * 128])
        lr_sb = sb("lr_sb", [128, 32 * 12])
        zrow = sb("zrow", [128, 32 * 12])
        zt1 = sb("zt1", [128, 32 * 12])
        ss_st = sb("ss_st", [1, 2048], BF16)
        n2acc = sb("n2acc", [128, 16])
        n2row = sb("n2row", [1, 16])
        cfrow = sb("cfrow", [1, 16])
        c2row = sb("c2row", [1, 16])
        cfcols = sb("cfcols", [128, 16])
        c2cols = sb("c2cols", [128, 16])
        meanrow = sb("meanrow", [1, 4])
        thrrow = sb("thrrow", [1, 4])
        thr4 = sb("thr4", [128, 4])
        nthr4 = sb("nthr4", [128, 4])
        red_sb = sb("red_sb", [1, 4])
        red_sb2 = sb("red_sb2", [1, 4])
        racc = sb("racc", [128, 40])
        junk = sb("junk", [128, 2048])

        p.dma("dsp", identb[:, :], ident_b[:, :], writes=[identb])
        p.dma("dsp", aI[:, :], aI_c[:, :], writes=[aI])
        p.dma("dsp", ones_c[:, :], ones_col[:, :], writes=[ones_c])
        p.dma("dsp", ones_r[:, :], ones_rowc[:, :], writes=[ones_r])
        p.dma("dpool", ones_rb[:, :], ones_rowc[:, :], writes=[ones_rb])
        p.dma("dsp", onw_r[:, :], onwd[:, :], writes=[onw_r])
        p.dma("dsp", blr_sb[:, :], blrd[:, :], writes=[blr_sb])
        p.dma("dsp", invn_sb[:, :], invn[:, :], writes=[invn_sb])
        p.op("pe", lambda e: e.matmul(psG[:, 0:1], onw_r[:, :], ones_r[0:1, 0:1],
                                      start=True, stop=True),
             reads=[onw_r, ones_r], writes=[psG])
        p.op("dve", lambda e: e.tensor_copy(onw_c[:, :], psG[:, 0:1]), reads=[psG], writes=[onw_c])
        p.op("pe", lambda e: e.matmul(psD[:, 0:48], ones_r[:, :], blr_sb[:, :],
                                      start=True, stop=True),
             reads=[ones_r, blr_sb], writes=[psD])
        p.op("dve", lambda e: e.tensor_scalar(blrB[:, :], psD[:, 0:48], BASE_LR_INV, None, ALU.add),
             reads=[psD], writes=[blrB])
        wfk = [(wf, (hh, j)) for hh in range(HPC) for j in range(3)]
        for hh in range(HPC):
            for j, wsrc in ((0, w0d), (1, w1d), (2, w2d)):
                col = (hh * 3 + j) * 128
                p.dma("dsp", wf[:, col:col + 128], wsrc[hh * 128:(hh + 1) * 128, :],
                      writes=[(wf, (hh, j))])
        p.op("dve", lambda e: e.tensor_copy(wbf[:, :], wf[:, :]), reads=wfk, writes=[wbf])
        for hh in range(HPC):
            colw = (hh * 3 + 1) * 128
            p.op("pe", lambda e, cw=colw: e.transpose(psT[:, :], wbf[:, cw:cw + 128], identb[:, :]),
                 reads=[wbf, identb], writes=[psT])
            p.op("dve", lambda e, hh=hh: e.tensor_copy(w1T[:, hh * 128:(hh + 1) * 128], psT[:, :]),
                 reads=[psT], writes=[(w1T, hh)])
        p.op("act", lambda e: e.activation(wsq[:, :], wf[:, :], AF.Square),
             reads=wfk, writes=[wsq])
        for kb in range(3):
            p.op("pe", lambda e, k=kb: e.matmul(psD[0:1, 0:512], ones_c[:, :],
                                                wsq[:, k * 512:(k + 1) * 512],
                                                start=True, stop=True),
                 reads=[ones_c, wsq], writes=[psD])
            p.op("dve", lambda e, k=kb: e.tensor_copy(ssw_row[0:1, k * 512:(k + 1) * 512],
                                                      psD[0:1, 0:512]),
                 reads=[psD], writes=[(ssw_row, kb)])
        sswk = [(ssw_row, kk) for kk in range(3)]
        p.op("act", lambda e: e.activation(wn_row[:, :], ssw_row[:, :], AF.Ln),
             reads=sswk, writes=[wn_row])
        p.op("act", lambda e: e.activation(wn_row[:, :], wn_row[:, :], AF.Exp, scale=0.5),
             reads=[wn_row], writes=[wn_row])

        mid1 = ExitStack()
        sb1 = lambda name, shape, dt=F32: mid1.enter_context(nc.sbuf_tensor(name, shape, dt))
        with mid1:
            wq_sb = sb1("wq_sb", [128, 16 * 1536], BF16)
            wlr_sb = sb1("wlr_sb", [128, 16 * 12], BF16)
            x_t = [sb1(f"x_t{i}", [128, D]) for i in range(4)]
            xn_b = [sb1(f"xn_b{i}", [128, D], BF16) for i in range(3)]
            xnfm = [sb1(f"xnfm{i}", [128, 16 * 512], BF16) for i in range(2)]
            sig_s = [sb1(f"sig_s{i}", [128, 512]) for i in range(2)]
            qk_s = [sb1(f"qk_s{i}", [128, 512]) for i in range(2)]
            qkb_s = [sb1(f"qkb_s{i}", [128, 512], BF16) for i in range(3)]
            p0a = [sb1(f"p0a{i}", [128, 1024]) for i in range(2)]
            qjb = sb1("qjb", [128, 512], BF16)
            sq2_b = [sb1(f"sq2_b{i}", [128, 512]) for i in range(2)]
            m_c = sb1("m_c", [128, 4])
            v_c = sb1("v_c", [128, 4])
            r_c = sb1("r_c", [128, 4])
            rr_c = sb1("rr_c", [128, 4])

            for kc in range(16):
                p.dma("dpool", wq_sb[:, kc * 1536:(kc + 1) * 1536],
                      wqkv[kc * 128:(kc + 1) * 128, :], writes=[(wq_sb, kc)])
                p.dma("dpool", wlr_sb[:, kc * 12:(kc + 1) * 12],
                      wlrd[kc * 128:(kc + 1) * 128, :], writes=[(wlr_sb, kc)])

            def ln_loads(s):
                for t4 in range(4):
                    xt = x_t[t4]
                    row0 = s * 512 + t4 * 128
                    p.dma(("dsp", "dact")[t4 % 2], xt[:, :], xin[row0:row0 + 128, :],
                          writes=[xt])

            def ln_stats(s, t4):
                    xt = x_t[t4]
                    p.op("act", lambda e, xt=xt, t4=t4: e.activation(
                        junk[:, :], xt[:, :], AF.Square, accum_out=v_c[:, t4:t4 + 1]),
                        reads=[xt], writes=[junk, (v_c, t4)])
                    p.op("dve", lambda e, xt=xt, t4=t4: e.reduce_sum(
                        m_c[:, t4:t4 + 1], xt[:, :], AX.X),
                        reads=[xt], writes=[(m_c, t4)])
                    p.op("dve", lambda e, t4=t4: e.tensor_scalar(
                        m_c[:, t4:t4 + 1], m_c[:, t4:t4 + 1], 1.0 / D, None, ALU.mult),
                        reads=[(m_c, t4)], writes=[(m_c, t4)])
                    p.op("dve", lambda e, t4=t4: e.tensor_tensor(
                        r_c[:, t4:t4 + 1], m_c[:, t4:t4 + 1], m_c[:, t4:t4 + 1], ALU.mult),
                        reads=[(m_c, t4)], writes=[(r_c, t4)])
                    p.op("dve", lambda e, t4=t4: e.scalar_tensor_tensor(
                        r_c[:, t4:t4 + 1], v_c[:, t4:t4 + 1], 1.0 / D, r_c[:, t4:t4 + 1],
                        ALU.mult, ALU.subtract),
                        reads=[(v_c, t4), (r_c, t4)], writes=[(r_c, t4)])

            def ln_finish(s):
                rck = [(r_c, t4) for t4 in range(4)]
                p.op("dve", lambda e: e.tensor_scalar(rr_c[:, :], r_c[:, :], 1e-5, None, ALU.add),
                     reads=rck, writes=[rr_c])
                p.op("act", lambda e: e.activation(rr_c[:, :], rr_c[:, :], AF.Ln),
                     reads=[rr_c], writes=[rr_c])
                p.op("act", lambda e: e.activation(rr_c[:, :], rr_c[:, :], AF.Exp, scale=-0.5),
                     reads=[rr_c], writes=[rr_c])
                spar = s % 2
                for t4 in range(4):
                    xt, xnb = x_t[t4], xn_b[t4 % 3]
                    row0 = s * 512 + t4 * 128
                    p.op("dve", lambda e, xt=xt, xnb=xnb, t4=t4: e.tensor_scalar(
                        xnb[:, :], xt[:, :], m_c[:, t4:t4 + 1], rr_c[:, t4:t4 + 1],
                        ALU.subtract, ALU.mult),
                        reads=[xt, (m_c, t4), rr_c], writes=[xnb])
                    p.dma(("dact", "dsp")[t4 % 2], xn_dram[row0:row0 + 128, :],
                          xnb[:, :], reads=[xnb], writes=[(xn_dram, (s, t4))])
                for kc in range(16):
                    p.dma("dsp", xnfm[spar][:, kc * 512:(kc + 1) * 512],
                          xn_dram[s * 512:(s + 1) * 512, kc * 128:(kc + 1) * 128],
                          reads=[(xn_dram, (s, t4)) for t4 in range(4)],
                          writes=[(xnfm[spar], (kc, t4)) for t4 in range(4)],
                          transpose=True)

            def emit_qkv(s, nxt, pwork=()):
                spar = s % 2
                pend = []
                pw = list(pwork)
                if nxt is not None:
                    ln_loads(nxt)
                for j in range(12):
                    if nxt is not None and j in (2, 4, 6, 8):
                        ln_stats(nxt, j // 2 - 1)
                    if nxt is not None and j == 10:
                        ln_finish(nxt)
                    if pw:
                        f = pw.pop(0)
                        if f:
                            f()
                    pj = [psA, psB, psC, psG, psH, psI][j % 6]
                    p.group("pe", [
                        (lambda e, pj=pj, j=j, kc=kc, spar=spar: e.matmul(
                            pj[:, 0:512],
                            wq_sb[:, kc * 1536 + j * 128: kc * 1536 + (j + 1) * 128],
                            xnfm[spar][:, kc * 512:(kc + 1) * 512],
                            start=(kc == 0), stop=(kc == 15)),
                         [(wq_sb, kc)] + [(xnfm[spar], (kc, t4)) for t4 in range(4)],
                         [pj])
                        for kc in range(16)],
                        reads=[(wq_sb, kc) for kc in range(16)] +
                              [(xnfm[spar], (kc, t4)) for kc in range(16)
                               for t4 in range(4)],
                        writes=[pj])
                    epar = j % 2
                    qpar = j % 3
                    p.op("act", lambda e, pj=pj, epar=epar: e.activation(
                        sig_s[epar][:, :], pj[:, 0:512], AF.Sigmoid),
                        reads=[pj], writes=[sig_s[epar]])
                    if j >= 8:
                        p.op("dve", lambda e, pj=pj, epar=epar, qpar=qpar: e.tensor_tensor(
                            qkb_s[qpar][:, :], pj[:, 0:512], sig_s[epar][:, :], ALU.mult),
                            reads=[pj, sig_s[epar]], writes=[qkb_s[qpar]])
                    else:
                        p.op("dve", lambda e, pj=pj, epar=epar: e.tensor_tensor(
                            qk_s[epar][:, :], pj[:, 0:512], sig_s[epar][:, :], ALU.mult),
                            reads=[pj, sig_s[epar]], writes=[qk_s[epar]])
                        p.op("dve", lambda e, epar=epar: e.tensor_tensor(
                            sq2_b[epar][:, :], qk_s[epar][:, :], qk_s[epar][:, :],
                            ALU.mult),
                            reads=[qk_s[epar]], writes=[sq2_b[epar]])
                        p.op("dve", lambda e, epar=epar, qpar=qpar: e.tensor_copy(
                            qkb_s[qpar][:, :], qk_s[epar][:, :]),
                            reads=[qk_s[epar]], writes=[qkb_s[qpar]])
                        pend.append((j, epar, qpar))
                    p.dma("dact", qkv_dram[j * 128:(j + 1) * 128, s * 512:(s + 1) * 512],
                          qkb_s[qpar][:, :], reads=[qkb_s[qpar]],
                          writes=[(qkv_dram, (j, s))])
                    while pend and (pend[0][0] < j or j == 11):
                        jd, ed, qd = pend.pop(0)
                        p.op("pe", lambda e, ed=ed: e.matmul(
                            psD[0:1, 0:512], ones_c[:, :], sq2_b[ed][:, :],
                            start=True, stop=True),
                            reads=[ones_c, sq2_b[ed]], writes=[psD])
                        p.op("dve", lambda e, qd=qd: e.tensor_copy(
                            ss_st[0:1, qd * 512:(qd + 1) * 512], psD[0:1, 0:512]),
                            reads=[psD], writes=[(ss_st, qd)])
                        p.dma("dact", ss_dram[jd:jd + 1, s * 512:(s + 1) * 512],
                              ss_st[0:1, qd * 512:(qd + 1) * 512],
                              reads=[(ss_st, qd)], writes=[(ss_dram, (jd, s))])
                for f in pw:
                    if f:
                        f()
                for tt in range(4):
                    p.group("pe", [
                        (lambda e, tt=tt, kc=kc, spar=spar: e.matmul(
                            psD[:, 448 + tt * 12: 448 + (tt + 1) * 12],
                            xnfm[spar][:, kc * 512 + tt * 128: kc * 512 + (tt + 1) * 128],
                            wlr_sb[:, kc * 12:(kc + 1) * 12],
                            start=(kc == 0), stop=(kc == 15)),
                         [(xnfm[spar], (kc, tt)), (wlr_sb, kc)], [psD])
                        for kc in range(16)],
                        reads=[(xnfm[spar], (kc, tt)) for kc in range(16)] +
                              [(wlr_sb, kc) for kc in range(16)],
                        writes=[psD])
                p.op("dve", lambda e, s=s: e.tensor_tensor(
                    zrow[:, s * 48:(s + 1) * 48], psD[:, 448:496], blrB[:, :], ALU.add),
                    reads=[psD, blrB], writes=[(zrow, s)])

            p0_tiles = _p0_tiles(wgud, wdnd)

            def abs_one(idx):
                return lambda: _p0_abs_part(nc, p, p0_tiles, idx, idx + 1,
                                            p0a, racc, junk)

            def quant_one(idx):
                return lambda: _p0_quant_part(nc, p, p0_tiles, idx, idx + 1,
                                              p0a, qjb, thr4, nthr4, wqn_g, wqn_d)

            ln_loads(0)
            for t4 in range(4):
                ln_stats(0, t4)
            ln_finish(0)
            for s in range(8):
                if s < 3:
                    pwk = [abs_one(i) for i in range((0, 13, 26)[s],
                                                     (13, 26, 38)[s])]
                    if s == 2:
                        pwk.append(lambda: _p0_reduce_a(
                            nc, p, red_in, red_out, red_sb, red_sb2, invn_sb,
                            meanrow, thrrow, racc, ones_c, psD))
                elif s == 3:
                    pwk = [None] * 5 + [lambda: _p0_reduce_b(
                        nc, p, thrrow, thr4, nthr4, ones_r, psD)]
                else:
                    pwk = [quant_one(i) for i in range((0, 10, 20, 29)[s - 4],
                                                       (10, 20, 29, 38)[s - 4])]
                emit_qkv(s, s + 1 if s + 1 < 8 else None, pwk)
            zread = [(zrow, s) for s in range(8)]
            p.op("act", lambda e: e.activation(zt1[:, :], zrow[:, :], AF.Abs),
                 reads=zread, writes=[zt1])
            p.op("act", lambda e: e.activation(zt1[:, :], zt1[:, :], AF.Exp, scale=-1.0),
                 reads=[zt1], writes=[zt1])
            p.op("act", lambda e: e.activation(zt1[:, :], zt1[:, :], AF.Ln, bias=1.0),
                 reads=[zt1], writes=[zt1])
            p.op("act", lambda e: e.activation(lr_sb[:, :], zrow[:, :], AF.Relu),
                 reads=zread, writes=[lr_sb])
            p.op("dve", lambda e: e.tensor_tensor(lr_sb[:, :], lr_sb[:, :], zt1[:, :], ALU.add),
                 reads=[lr_sb, zt1], writes=[lr_sb])

        p.barrier()

        mid2 = ExitStack()
        sb2 = lambda name, shape, dt=F32: mid2.enter_context(nc.sbuf_tensor(name, shape, dt))
        with mid2:
            qc_b = [sb2(f"qc_b{i}", [128, HPC * 512], BF16) for i in range(2)]
            kc_b = [sb2(f"kc_b{i}", [128, HPC * 512], BF16) for i in range(2)]
            vc_b = [sb2(f"vc_b{i}", [128, HPC * 512], BF16) for i in range(2)]
            ktvt = [sb2(f"ktvt{i}", [128, 4 * 1024], BF16) for i in range(2)]
            gate_f = [sb2(f"gate_f{i}", [128, 512]) for i in range(2)]
            s_f = [sb2(f"s_f{i}", [128, 512]) for i in range(2)]
            hb_f = [sb2(f"hb_f{i}", [128, 512]) for i in range(2)]
            dh_f = [sb2(f"dh_f{i}", [128, 512]) for i in range(2)]
            sg_f = [sb2(f"sg_f{i}", [128, 512]) for i in range(2)]
            t1_f = [sb2(f"t1_f{i}", [128, 512]) for i in range(2)]
            hidl_b = [sb2(f"hidl_b{i}", [128, 512], BF16) for i in range(4)]
            k0t_b = [sb2(f"k0t_b{i}", [128, 512], BF16) for i in range(4)]
            k2t_b = [sb2(f"k2t_b{i}", [128, 512], BF16) for i in range(4)]
            dgba_b = [sb2(f"dgba_b{i}", [128, 512], BF16) for i in range(4)]
            dhbm_b = [sb2(f"dhbm_b{i}", [128, 512], BF16) for i in range(4)]
            Gb = sb2("Gb", [128, 12 * 128], BF16)
            GTb = sb2("GTb", [128, 12 * 128], BF16)
            ns_scr = [sb2(f"ns_scr{i}", [128, 2 * 128]) for i in range(3)]
            ns_bf = [sb2(f"ns_bf{i}", [128, 4 * 128], BF16) for i in range(3)]
            A0b12 = sb2("A0b12", [128, 12 * 128], BF16)
            Mb12 = sb2("Mb12", [128, 12 * 128], BF16)
            sqg = [sb2(f"sqg{i}", [128, 128]) for i in range(3)]
            rkcol = sb2("rkcol", [128, 16])
            klcol = sb2("klcol", [128, 32])
            eb_b = [sb2(f"eb_b{i}", [128, 512], BF16) for i in range(2)]
            oi_b = [sb2(f"oi_b{i}", [128, 512], BF16) for i in range(4)]
            ssc = [sb2(f"ssc{i}", [1, 4096], BF16) for i in range(2)]
            scrow = sb2("scrow", [1, 12 * 128])
            p2s = ExitStack()
            sb2b = lambda name, shape, dt=F32: p2s.enter_context(nc.sbuf_tensor(name, shape, dt))
            trbuf = sb2b("trbuf", [128, 2816], BF16)

            def norms_for_chunk(c):
                par = c % 2
                buf = ssc[par]
                rd = [(ss_dram, (r, c)) for r in range(8)]
                p.dma("dsp", buf[0:1, :].rearrange("p (r t) -> p r t", r=8),
                      ss_dram[0:8, c * 512:(c + 1) * 512], reads=rd, writes=[buf])
                p.op("act", lambda e, buf=buf: e.activation(buf[:, :], buf[:, :], AF.Ln),
                     reads=[buf], writes=[buf])
                p.op("act", lambda e, buf=buf: e.activation(buf[:, :], buf[:, :],
                                                            AF.Exp, scale=0.5),
                     reads=[buf], writes=[buf])
                p.op("dve", lambda e, buf=buf: e.tensor_scalar(buf[:, :], buf[:, :],
                                                               1e-5, None, ALU.add),
                     reads=[buf], writes=[buf])
                def _recip(e, buf=buf):
                    with nc.allow_low_precision("bf16 q/k norm factors"):
                        return e.reciprocal(buf[:, :], buf[:, :])
                p.op("dve", _recip, reads=[buf], writes=[buf])

            def prefetch_chunk(c):
                par = c % 2
                for which, buf in ((0, qc_b[par]), (1, kc_b[par]), (2, vc_b[par])):
                    sap = qkv_dram[which * 512:(which + 1) * 512, c * 512:(c + 1) * 512]
                    sap = sap.rearrange("(h d) t -> d h t", d=128)
                    dap = buf[:, :].rearrange("d (h t) -> d h t", h=4)
                    p.dma("dsp", dap, sap,
                          reads=[(qkv_dram, (which * 4 + hh, c)) for hh in range(4)],
                          writes=[buf])
                for tt in range(4):
                    p.dma("dsp", ktvt[par][:, tt * 1024:(tt + 1) * 1024],
                          qkv_dram[4 * 128:12 * 128, c * 512 + tt * 128: c * 512 + (tt + 1) * 128],
                          reads=[(qkv_dram, (j, c)) for j in range(4, 12)],
                          writes=[(ktvt[par], tt)], transpose=True)

            norms_for_chunk(0)
            prefetch_chunk(0)

            for c in range(NCH):
                par = c % 2
                qcb, kcb, vcb, kv = qc_b[par], kc_b[par], vc_b[par], ktvt[par]
                if c + 1 < NCH:
                    prefetch_chunk(c + 1)

                for hh in range(HPC):
                    for tt in range(4):
                        cid = hh * 4 + tt
                        off = (4 + hh) * 512 + tt * 128
                        p.op("pe", lambda e, off=off, cid=cid, par=par: e.matmul(
                            psD[:, cid:cid + 1], ssc[par][0:1, off:off + 128],
                            ones_rb[0:1, 0:1], start=True, stop=True),
                            reads=[ssc[par], ones_rb], writes=[psD])
                p.op("dve", lambda e: e.tensor_copy(rkcol[:, :], psD[:, 0:16]),
                     reads=[psD], writes=[rkcol])
                lrview = lr_sb[:, 48 * c:48 * c + 48].rearrange("p (t J) -> p J t", t=4)
                rkv = rkcol[:, 0:16].rearrange("p (h t) -> p h t", h=4)
                p.op("dve", lambda e, lrview=lrview, rkv=rkv: e.tensor_tensor(
                    klcol[:, 0:16].rearrange("p (h t) -> p h t", h=4),
                    rkv, lrview[:, 0:4, :], ALU.mult),
                    reads=[rkcol, lr_sb], writes=[(klcol, "l0")])
                p.op("dve", lambda e, lrview=lrview, rkv=rkv: e.tensor_tensor(
                    klcol[:, 16:32].rearrange("p (h t) -> p h t", h=4),
                    rkv, lrview[:, 8:12, :], ALU.mult),
                    reads=[rkcol, lr_sb], writes=[(klcol, "l2")])

                for hh in range(HPC):
                    hpar = hh
                    gf, sf, hbf, dhf, sgf, t1f = (gate_f[hh % 2], s_f[hh % 2], hb_f[hh % 2],
                                                  dh_f[hh % 2], sg_f[hh % 2], t1_f[hh % 2])
                    for tt in range(4):
                        ksl = kcb[:, hh * 512 + tt * 128: hh * 512 + (tt + 1) * 128]
                        vsl = vcb[:, hh * 512 + tt * 128: hh * 512 + (tt + 1) * 128]
                        p.op("pe", lambda e, ksl=ksl, tt=tt, hh=hh: e.matmul(
                            psA[:, tt * 128:(tt + 1) * 128], ksl,
                            wbf[:, (hh * 3 + 0) * 128:(hh * 3 + 1) * 128],
                            start=True, stop=True),
                            reads=[kcb, wbf], writes=[psA])
                        p.op("pe", lambda e, ksl=ksl, tt=tt, hh=hh: e.matmul(
                            psB[:, tt * 128:(tt + 1) * 128], ksl,
                            wbf[:, (hh * 3 + 2) * 128:(hh * 3 + 3) * 128],
                            start=True, stop=True),
                            reads=[kcb, wbf], writes=[psB])
                        p.op("pe", lambda e, vsl=vsl, tt=tt, hh=hh: e.matmul(
                            psC[:, tt * 128:(tt + 1) * 128], vsl,
                            w1T[:, hh * 128:(hh + 1) * 128],
                            start=True, stop=True),
                            reads=[vcb, (w1T, hh)], writes=[psC])
                    for tt in range(4):
                        cid = hh * 4 + tt
                        sl = slice(tt * 128, (tt + 1) * 128)
                        p.op("act", lambda e, sl=sl, cid=cid, gf=gf: e.activation(
                            gf[:, sl], psA[:, sl], AF.Copy, scale=rkcol[:, cid:cid + 1]),
                            reads=[psA, rkcol], writes=[gf])
                        p.op("act", lambda e, sl=sl, cid=cid, sf=sf: e.activation(
                            sf[:, sl], psA[:, sl], AF.Sigmoid, scale=rkcol[:, cid:cid + 1]),
                            reads=[psA, rkcol], writes=[sf])
                        p.op("act", lambda e, sl=sl, cid=cid, hbf=hbf: e.activation(
                            hbf[:, sl], psB[:, sl], AF.Copy, scale=rkcol[:, cid:cid + 1]),
                            reads=[psB, rkcol], writes=[hbf])
                    p.op("act", lambda e, dhf=dhf: e.activation(dhf[:, :], psC[:, 0:512], AF.Copy),
                         reads=[psC], writes=[dhf])
                    p.op("dve", lambda e, gf=gf, sf=sf, sgf=sgf: e.tensor_tensor(
                        sgf[:, :], gf[:, :], sf[:, :], ALU.mult),
                        reads=[gf, sf], writes=[sgf])
                    for tt in range(4):
                        l1 = (4 * c + tt) * 12 + 4 + hh
                        slq = slice(tt * 128, (tt + 1) * 128)
                        p.op("dve", lambda e, sgf=sgf, hbf=hbf, slq=slq, l1=l1, hpar=hpar:
                             e.scalar_tensor_tensor(
                                 hidl_b[hpar][:, slq], sgf[:, slq],
                                 lr_sb[:, l1:l1 + 1], hbf[:, slq], ALU.mult, ALU.mult),
                             reads=[sgf, hbf, lr_sb], writes=[(hidl_b[hpar], tt)])
                    p.op("dve", lambda e, dhf=dhf, sgf=sgf, hpar=hpar: e.tensor_tensor(
                        dhbm_b[hpar][:, :], dhf[:, :], sgf[:, :], ALU.mult),
                        reads=[dhf, sgf], writes=[dhbm_b[hpar]])
                    p.op("dve", lambda e, dhf=dhf, hbf=hbf, t1f=t1f: e.tensor_tensor(
                        t1f[:, :], dhf[:, :], hbf[:, :], ALU.mult),
                        reads=[dhf, hbf], writes=[t1f])
                    p.op("dve", lambda e, sgf=sgf, sf=sf, dhf=dhf: e.tensor_tensor(
                        dhf[:, :], sgf[:, :], sf[:, :], ALU.mult),
                        reads=[sgf, sf], writes=[dhf])
                    p.op("dve", lambda e, sgf=sgf, sf=sf: e.tensor_tensor(
                        sgf[:, :], sgf[:, :], sf[:, :], ALU.add),
                        reads=[sgf, sf], writes=[sgf])
                    p.op("dve", lambda e, sgf=sgf, dhf=dhf: e.tensor_tensor(
                        sgf[:, :], sgf[:, :], dhf[:, :], ALU.subtract),
                        reads=[sgf, dhf], writes=[sgf])
                    p.op("dve", lambda e, t1f=t1f, sgf=sgf, hpar=hpar: e.tensor_tensor(
                        dgba_b[hpar][:, :], t1f[:, :], sgf[:, :], ALU.mult),
                        reads=[t1f, sgf], writes=[dgba_b[hpar]])
                    for tt in range(4):
                        cid = hh * 4 + tt
                        ktap = kv[:, tt * 1024 + hh * 128: tt * 1024 + (hh + 1) * 128]
                        p.op("dve", lambda e, ktap=ktap, cid=cid, tt=tt, hpar=hpar: e.tensor_scalar(
                            k0t_b[hpar][:, tt * 128:(tt + 1) * 128], ktap,
                            klcol[:, cid:cid + 1], None, ALU.mult),
                            reads=[(ktvt[par], tt), (klcol, "l0")], writes=[(k0t_b[hpar], tt)])
                        p.op("dve", lambda e, ktap=ktap, cid=cid, tt=tt, hpar=hpar: e.tensor_scalar(
                            k2t_b[hpar][:, tt * 128:(tt + 1) * 128], ktap,
                            klcol[:, 16 + cid:17 + cid], None, ALU.mult),
                            reads=[(ktvt[par], tt), (klcol, "l2")], writes=[(k2t_b[hpar], tt)])

                def grad(pX, lhs, lhsk, rhs, rhsk, gidx, GT, hh):
                    items = []
                    for tt in range(4):
                        lap = lhs(tt) if callable(lhs) else lhs[:, tt * 128:(tt + 1) * 128]
                        rap = rhs(tt) if callable(rhs) else rhs[:, tt * 128:(tt + 1) * 128]
                        items.append((lambda e, pX=pX, lap=lap, rap=rap, tt=tt: e.matmul(
                            pX[:, 0:128], lap, rap, start=(tt == 0), stop=(tt == 3)),
                            lhsk + rhsk, [pX]))
                    p.group("pe", items, reads=lhsk + rhsk, writes=[pX])
                    dst = GTb if GT else Gb
                    p.op("act", lambda e, pX=pX, dst=dst, gidx=gidx: e.activation(
                        dst[:, gidx * 128:(gidx + 1) * 128], pX[:, 0:128], AF.Copy),
                        reads=[pX], writes=[(dst, gidx)])
                    if not GT:
                        sqx = sqg[gidx % 3]
                        p.op("act", lambda e, pX=pX, gidx=gidx, sqx=sqx: e.activation(
                            sqx[:, :], pX[:, 0:128], AF.Square,
                            accum_out=n2acc[:, gidx:gidx + 1]),
                            reads=[pX], writes=[sqx, (n2acc, gidx)])

                gq = 0
                for hh in range(HPC):
                    hl, k0, k2 = hidl_b[hh], k0t_b[hh], k2t_b[hh]
                    dga, dhb = dgba_b[hh], dhbm_b[hh]
                    hlk = [(hl, tt) for tt in range(4)]
                    k0k = [(k0, tt) for tt in range(4)]
                    k2k = [(k2, tt) for tt in range(4)]
                    kvk = [(ktvt[par], tt) for tt in range(4)]

                    def vtile(tt, hh=hh, kv=kv):
                        return kv[:, tt * 1024 + 512 + hh * 128: tt * 1024 + 512 + (hh + 1) * 128]

                    g0 = hh * 3
                    grad(psm[gq % 3], k0, k0k, dga, [dga], g0, False, hh); gq += 1
                    grad(psm[gq % 3], dga, [dga], k0, k0k, g0, True, hh); gq += 1
                    grad(psm[gq % 3], hl, hlk, vtile, kvk, g0 + 1, False, hh); gq += 1
                    grad(psm[gq % 3], vtile, kvk, hl, hlk, g0 + 1, True, hh); gq += 1
                    grad(psm[gq % 3], k2, k2k, dhb, [dhb], g0 + 2, False, hh); gq += 1
                    grad(psm[gq % 3], dhb, [dhb], k2, k2k, g0 + 2, True, hh); gq += 1

                p.op("pe", lambda e: e.matmul(psD[0:1, 0:12], ones_c[:, :], n2acc[:, 0:12],
                                              start=True, stop=True),
                     reads=[ones_c] + [(n2acc, g) for g in range(12)], writes=[psD])
                p.op("dve", lambda e: e.tensor_copy(n2row[0:1, 0:12], psD[0:1, 0:12]),
                     reads=[psD], writes=[n2row])
                p.op("act", lambda e: e.activation(cfrow[0:1, 0:12], n2row[0:1, 0:12], AF.Ln),
                     reads=[n2row], writes=[cfrow])
                p.op("act", lambda e: e.activation(cfrow[0:1, 0:12], cfrow[0:1, 0:12],
                                                   AF.Exp, scale=0.5),
                     reads=[cfrow], writes=[cfrow])
                p.op("dve", lambda e: e.tensor_scalar(cfrow[0:1, 0:12], cfrow[0:1, 0:12],
                                                      1e-7, None, ALU.add),
                     reads=[cfrow], writes=[cfrow])
                p.op("dve", lambda e: e.reciprocal(cfrow[0:1, 0:12], cfrow[0:1, 0:12]),
                     reads=[cfrow], writes=[cfrow])
                p.op("dve", lambda e: e.tensor_tensor(c2row[0:1, 0:12], cfrow[0:1, 0:12],
                                                      cfrow[0:1, 0:12], ALU.mult),
                     reads=[cfrow], writes=[c2row])
                p.op("pe", lambda e: e.matmul(psD[:, 32:44], ones_r[:, :], cfrow[0:1, 0:12],
                                              start=True, stop=True),
                     reads=[ones_r, cfrow], writes=[psD])
                p.op("dve", lambda e: e.tensor_copy(cfcols[:, 0:12], psD[:, 32:44]),
                     reads=[psD], writes=[cfcols])
                p.op("pe", lambda e: e.matmul(psD[:, 48:60], ones_r[:, :], c2row[0:1, 0:12],
                                              start=True, stop=True),
                     reads=[ones_r, c2row], writes=[psD])
                p.op("dve", lambda e: e.tensor_copy(c2cols[:, 0:12], psD[:, 48:60]),
                     reads=[psD], writes=[c2cols])
                if c + 1 < NCH:
                    norms_for_chunk(c + 1)

                def ns_chain(gi):
                    hh, j = gi // 3, gi % 3
                    scrX = ns_scr[gi % 3]
                    bfX = ns_bf[gi % 3]
                    Gap = Gb[:, gi * 128:(gi + 1) * 128]
                    GTap = GTb[:, gi * 128:(gi + 1) * 128]
                    A0b = A0b12[:, gi * 128:(gi + 1) * 128]
                    Mb = Mb12[:, gi * 128:(gi + 1) * 128]
                    Ub, A1b = bfX[:, 0:128], bfX[:, 128:256]
                    M1b, X1b = bfX[:, 256:384], bfX[:, 384:512]
                    cS, mS = scrX[:, 0:128], scrX[:, 128:256]
                    cfc = cfcols[:, gi:gi + 1]
                    c2c = c2cols[:, gi:gi + 1]
                    wap = wf[:, gi * 128:(gi + 1) * 128]
                    wkey = (wf, (hh, j))
                    return [
                        ("mm", lambda pX: (lambda e: e.matmul(pX, GTap, GTap,
                                                              start=True, stop=True),
                                           [(GTb, gi)], [pX])),
                        ("ev", lambda pX: ("act", lambda e: e.activation(
                            A0b, pX, AF.Copy, scale=c2c),
                            [pX, c2cols], [(A0b12, gi)])),
                        ("mm", lambda pX: (lambda e: e.matmul(pX, A0b, A0b,
                                                              start=True, stop=True),
                                           [(A0b12, gi)], [pX])),
                        ("ev", lambda pX: ("dve", lambda e: e.tensor_scalar(
                            cS, pX, NS_C, None, ALU.mult),
                            [pX], [(scrX, 0)])),
                        ("pl", ("dve", lambda e: e.scalar_tensor_tensor(
                            mS, A0b, NS_B, cS, ALU.mult, ALU.add),
                            [(A0b12, gi), (scrX, 0)], [(scrX, 1)])),
                        ("pl", ("dve", lambda e: e.tensor_tensor(Mb, mS, aI[:, :], ALU.add),
                                [(scrX, 1), aI], [(Mb12, gi)])),
                        ("mm", lambda pX: (lambda e: e.matmul(pX, A0b, Mb,
                                                              start=True, stop=True),
                                           [(A0b12, gi), (Mb12, gi)], [pX])),
                        ("ev", lambda pX: ("act", lambda e: e.activation(
                            Ub, pX, AF.Copy), [pX], [(bfX, 0)])),
                        ("mm", lambda pX: (lambda e: e.matmul(pX, Mb, Ub,
                                                              start=True, stop=True),
                                           [(Mb12, gi), (bfX, 0)], [pX])),
                        ("ev", lambda pX: ("dve", lambda e: e.tensor_copy(A1b, pX),
                                           [pX], [(bfX, 1)])),
                        ("mm", lambda pX: (lambda e: e.matmul(pX, A1b, A1b,
                                                              start=True, stop=True),
                                           [(bfX, 1)], [pX])),
                        ("ev", lambda pX: ("act", lambda e: e.activation(
                            cS, pX, AF.Copy, scale=NS_C), [pX], [(scrX, 0)])),
                        ("pl", ("dve", lambda e: e.scalar_tensor_tensor(
                            mS, A1b, NS_B, cS, ALU.mult, ALU.add),
                            [(bfX, 1), (scrX, 0)], [(scrX, 1)])),
                        ("pl", ("dve", lambda e: e.tensor_tensor(M1b, mS, aI[:, :], ALU.add),
                                [(scrX, 1), aI], [(bfX, 2)])),
                        ("mm", lambda pX: (lambda e: e.matmul(pX, Mb, Gap,
                                                              start=True, stop=True),
                                           [(Mb12, gi), (Gb, gi)], [pX])),
                        ("ev", lambda pX: ("act", lambda e: e.activation(
                            X1b, pX, AF.Copy, scale=cfc),
                            [pX, cfcols], [(bfX, 3)])),
                        ("mm", lambda pX: (lambda e: e.matmul(pX, M1b, X1b,
                                                              start=True, stop=True),
                                           [(bfX, 2), (bfX, 3)], [pX])),
                        ("ev", lambda pX: ("dve", lambda e: e.tensor_tensor(
                            wap, wap, pX, ALU.add), [pX, wkey], [wkey])),
                    ]

                chains = [ns_chain(gi) for gi in range(12)]
                sched = []
                for gi in range(12):
                    for s in range(18):
                        sched.append((s + 100 * (gi // 3), gi, s))
                sched.sort(key=lambda x: (x[0], x[1]))
                cur_pX = {}
                mmctr = 0

                def _sub(lst, ap, hnd):
                    return [hnd if x is ap else x for x in lst]

                for _, gi, s in sched:
                    kind, item = chains[gi][s]
                    if kind == "mm":
                        hnd = psm[mmctr % 3]
                        mmctr += 1
                        ap = hnd[:, 0:128]
                        cur_pX[gi] = (hnd, ap)
                        fn, r, wr = item(ap)
                        p.op("pe", fn, reads=_sub(r, ap, hnd), writes=_sub(wr, ap, hnd))
                    elif kind == "ev":
                        hnd, ap = cur_pX[gi]
                        eng, fn, r, wr = item(ap)
                        p.op(eng, fn, reads=_sub(r, ap, hnd), writes=_sub(wr, ap, hnd))
                    else:
                        eng, fn, r, wr = item
                        p.op(eng, fn, reads=r, writes=wr)

                p.op("act", lambda e: e.activation(wsq[:, :], wf[:, :], AF.Square),
                     reads=wfk, writes=[wsq])
                for kb in range(3):
                    p.op("pe", lambda e, k=kb: e.matmul(psD[0:1, 0:512], ones_c[:, :],
                                                        wsq[:, k * 512:(k + 1) * 512],
                                                        start=True, stop=True),
                         reads=[ones_c, wsq], writes=[psD])
                    p.op("dve", lambda e, k=kb: e.tensor_copy(
                        ssw_row[0:1, k * 512:(k + 1) * 512], psD[0:1, 0:512]),
                        reads=[psD], writes=[(ssw_row, kb)])
                p.op("act", lambda e: e.activation(scrow[:, :], ssw_row[:, :], AF.Ln),
                     reads=sswk, writes=[scrow])
                p.op("act", lambda e: e.activation(scrow[:, :], scrow[:, :], AF.Exp, scale=0.5),
                     reads=[scrow], writes=[scrow])
                p.op("dve", lambda e: e.tensor_scalar(scrow[:, :], scrow[:, :], 1e-5, None, ALU.add),
                     reads=[scrow], writes=[scrow])
                p.op("dve", lambda e: e.reciprocal(scrow[:, :], scrow[:, :]),
                     reads=[scrow], writes=[scrow])
                p.op("dve", lambda e: e.tensor_tensor(scrow[:, :], scrow[:, :], wn_row[:, :],
                                                      ALU.mult),
                     reads=[scrow, wn_row], writes=[scrow])
                for kb in range(3):
                    p.op("pe", lambda e, k=kb: e.matmul(
                        psD[:, 0:512], ones_r[:, :], scrow[0:1, k * 512:(k + 1) * 512],
                        start=True, stop=True),
                        reads=[ones_r, scrow], writes=[psD])
                    for q4 in range(4):
                        gi = kb * 4 + q4
                        p.op("dve", lambda e, gi=gi, q4=q4: e.tensor_tensor(
                            wf[:, gi * 128:(gi + 1) * 128], wf[:, gi * 128:(gi + 1) * 128],
                            psD[:, q4 * 128:(q4 + 1) * 128], ALU.mult),
                            reads=[psD, (wf, (gi // 3, gi % 3))],
                            writes=[(wf, (gi // 3, gi % 3))])
                p.op("dve", lambda e: e.tensor_copy(wbf[:, :], wf[:, :]),
                     reads=wfk, writes=[wbf])
                for hh in range(HPC):
                    colw = (hh * 3 + 1) * 128
                    p.op("pe", lambda e, cw=colw: e.transpose(psT[:, :], wbf[:, cw:cw + 128],
                                                              identb[:, :]),
                         reads=[wbf, identb], writes=[psT])
                    p.op("dve", lambda e, hh=hh: e.tensor_copy(
                        w1T[:, hh * 128:(hh + 1) * 128], psT[:, :]),
                        reads=[psT], writes=[(w1T, hh)])

                for hh in range(HPC):
                    hpar = hh % 2
                    p.op("pe", lambda e, hh=hh, par=par: e.matmul(
                        psC[:, 0:512], ones_rb[:, :],
                        ssc[par][0:1, hh * 512:(hh + 1) * 512],
                        start=True, stop=True),
                        reads=[ones_rb, ssc[par]], writes=[psC])
                    p.op("dve", lambda e, hh=hh, qcb=qcb: e.tensor_tensor(
                        qcb[:, hh * 512:(hh + 1) * 512], qcb[:, hh * 512:(hh + 1) * 512],
                        psC[:, 0:512], ALU.mult),
                        reads=[psC, qcb], writes=[qcb])
                    p.op("pe", lambda e, hh=hh, qcb=qcb: e.matmul(
                        psA[:, 0:512], wbf[:, (hh * 3 + 0) * 128:(hh * 3 + 1) * 128],
                        qcb[:, hh * 512:(hh + 1) * 512], start=True, stop=True),
                        reads=[wbf, qcb], writes=[psA])
                    p.op("pe", lambda e, hh=hh, qcb=qcb: e.matmul(
                        psB[:, 0:512], wbf[:, (hh * 3 + 2) * 128:(hh * 3 + 3) * 128],
                        qcb[:, hh * 512:(hh + 1) * 512], start=True, stop=True),
                        reads=[wbf, qcb], writes=[psB])
                    gf, sf = gate_f[hpar], s_f[hpar]
                    p.op("act", lambda e, gf=gf: e.activation(gf[:, :], psA[:, 0:512], AF.Copy),
                         reads=[psA], writes=[gf])
                    p.op("act", lambda e, sf=sf: e.activation(sf[:, :], psA[:, 0:512], AF.Sigmoid),
                         reads=[psA], writes=[sf])
                    p.op("dve", lambda e, gf=gf, sf=sf: e.tensor_tensor(
                        gf[:, :], gf[:, :], sf[:, :], ALU.mult),
                        reads=[gf, sf], writes=[gf])
                    p.op("dve", lambda e, gf=gf, hpar=hpar: e.tensor_tensor(
                        eb_b[hpar][:, :], gf[:, :], psB[:, 0:512], ALU.mult),
                        reads=[gf, psB], writes=[eb_b[hpar]])
                    p.op("pe", lambda e, hh=hh, hpar=hpar: e.matmul(
                        psC[:, 0:512], wbf[:, (hh * 3 + 1) * 128:(hh * 3 + 2) * 128],
                        eb_b[hpar][:, :], start=True, stop=True),
                        reads=[wbf, eb_b[hpar]], writes=[psC])
                    p.op("act", lambda e, hh=hh: e.activation(
                        oi_b[hh][:, :], psC[:, 0:512], AF.Copy),
                        reads=[psC], writes=[oi_b[hh]])
                    p.dma("dact", o_dram[hh * 128:(hh + 1) * 128, c * 512:(c + 1) * 512],
                          oi_b[hh][:, :], reads=[oi_b[hh]],
                          writes=[(o_dram, (hh, c))])

                if c == 5:
                    _p0_transpose(nc, p, wqn_g, wqn_d, wTg_p, wTd_p, wTg, wTd, trbuf)

            p.barrier()
            p2s.close()
            ob = [sb2(f"ob{i}", [128, L], BF16) for i in range(2)]
            osq = sb2("osq", [128, 512])
            ssq3 = sb2("ssq3", [1, L], BF16)
            srb = [sb2(f"srb{i}", [128, 512], BF16) for i in range(2)]
            for hh in range(HPC):
                par = hh % 2
                p.dma("dsp", ob[par][:, :], o_dram[hh * 128:(hh + 1) * 128, :],
                      reads=[(o_dram, (hh, c)) for c in range(NCH)], writes=[ob[par]])
                for s8 in range(8):
                    p.op("act", lambda e, par=par, s8=s8: e.activation(
                        osq[:, :], ob[par][:, s8 * 512:(s8 + 1) * 512], AF.Square),
                        reads=[ob[par]], writes=[osq])
                    p.op("pe", lambda e: e.matmul(psD[0:1, 0:512], ones_c[:, :], osq[:, :],
                                                  start=True, stop=True),
                         reads=[ones_c, osq], writes=[psD])
                    p.op("dve", lambda e, s8=s8: e.tensor_copy(
                        ssq3[0:1, s8 * 512:(s8 + 1) * 512], psD[0:1, 0:512]),
                        reads=[psD], writes=[(ssq3, s8)])
                rk3 = [(ssq3, s8) for s8 in range(8)]
                p.op("dve", lambda e: e.tensor_scalar(
                    ssq3[:, :], ssq3[:, :], 1.0 / 128, 1e-5, ALU.mult, ALU.add),
                    reads=rk3, writes=rk3)
                p.op("act", lambda e: e.activation(ssq3[:, :], ssq3[:, :], AF.Ln),
                    reads=rk3, writes=rk3)
                p.op("act", lambda e: e.activation(
                    ssq3[:, :], ssq3[:, :], AF.Exp, scale=-0.5),
                    reads=rk3, writes=rk3)
                for s8 in range(8):
                    spar = s8 % 2
                    p.op("pe", lambda e, s8=s8: e.matmul(
                        psD[:, 0:512], ones_rb[:, :],
                        ssq3[0:1, s8 * 512:(s8 + 1) * 512],
                        start=True, stop=True),
                        reads=[ones_rb, (ssq3, s8)], writes=[psD])
                    p.op("dve", lambda e, par=par, s8=s8, spar=spar: e.scalar_tensor_tensor(
                        srb[spar][:, :], ob[par][:, s8 * 512:(s8 + 1) * 512],
                        onw_c[:, 0:1], psD[:, 0:512], ALU.mult, ALU.mult),
                        reads=[ob[par], onw_c, psD], writes=[srb[spar]])
                    p.dma("dact",
                          xch_a2a[hh * 128:(hh + 1) * 128, s8 * 512:(s8 + 1) * 512],
                          srb[spar][:, :], reads=[srb[spar]],
                          writes=[(xch_a2a, (hh, s8))])
            p.cc("AllGather", ALU.bypass, G44, [xch_a2a[:, :]], [xch_r[:, :]],
                 reads=[(xch_a2a, (hh, s8)) for hh in range(HPC) for s8 in range(8)],
                 writes=[xch_r])

        p.barrier()

        f4 = ExitStack()
        sb4 = lambda name, shape, dt=F32: f4.enter_context(nc.sbuf_tensor(name, shape, dt))
        with f4:
            nw_r = sb4("nw_r", [1, 512])
            nwB = sb4("nwB", [128, D])
            g_sb = sb4("g_sb", [128, 8 * D], BF16)
            xo_t = [sb4(f"xo_t{i}", [128, D]) for i in range(2)]
            c1cols = sb4("c1cols", [128, 8])
            c2cols2 = sb4("c2cols2", [128, 8])
            am2c = sb4("am2c", [128, 8])
            mgu_c = sb4("mgu_c", [128, 2])
            am_c = sb4("am_c", [128, 12])
            sc_c = sb4("sc_c", [128, 2])
            fm_c = sb4("fm_c", [128, 4])
            fv_c = sb4("fv_c", [128, 4])
            fr_c = sb4("fr_c", [128, 4])
            fr2_c = sb4("fr2_c", [128, 4])
            ft_c = sb4("ft_c", [128, 4])
            sgf2 = [sb4(f"sgf2{i}", [128, 512], BF16) for i in range(2)]

            for nb in range(4):
                p.dma("dsp", nw_r[:, :], nwd[0:1, nb * 512:(nb + 1) * 512],
                      writes=[nw_r])
                p.op("pe", lambda e, nb=nb: e.matmul(
                    psD[:, 0:512], ones_r[:, :], nw_r[0:1, 0:512],
                    start=True, stop=True),
                    reads=[ones_r, nw_r], writes=[psD])
                p.op("dve", lambda e, nb=nb: e.tensor_copy(
                    nwB[:, nb * 512:(nb + 1) * 512], psD[:, 0:512]),
                    reads=[psD], writes=[(nwB, nb)])
            nwk = [(nwB, nb) for nb in range(4)]
            p.op("pe", lambda e: e.matmul(psD[:, 96:98], ones_r[:, :], meanrow[0:1, 0:2],
                                          start=True, stop=True),
                 reads=[ones_r, meanrow], writes=[psD])
            p.op("dve", lambda e: e.tensor_copy(mgu_c[:, :], psD[:, 96:98]),
                 reads=[psD], writes=[mgu_c])

            p.barrier()
            q4s = ExitStack()
            sbq = lambda name, shape, dt=F32: q4s.enter_context(
                nc.sbuf_tensor(name, shape, dt))
            with q4s:
                xnq = [sbq(f"xnq_{i}", [128, D]) for i in range(2)]
                xqb = [sbq(f"xqb_{i}", [128, D], BF16) for i in range(2)]
                xqi = sbq("xqi", [128, D], I32)
                for half in range(2):
                        base = half * 512
                        for tt in range(4):
                            par = tt % 2
                            xt = xo_t[par]
                            grow = base + tt * 128
                            p.dma("dsp", xt[:, :], xqin[grow:grow + 128, :], writes=[xt])
                            p.op("act", lambda e, xt=xt, tt=tt: e.activation(
                                junk[:, :], xt[:, :], AF.Square, accum_out=fv_c[:, tt:tt + 1]),
                                reads=[xt], writes=[junk, (fv_c, tt)])
                            p.op("dve", lambda e, xt=xt, tt=tt: e.reduce_sum(
                                fm_c[:, tt:tt + 1], xt[:, :], AX.X),
                                reads=[xt], writes=[(fm_c, tt)])
                            p.op("dve", lambda e, tt=tt: e.tensor_scalar(
                                fm_c[:, tt:tt + 1], fm_c[:, tt:tt + 1], 1.0 / D, None, ALU.mult),
                                reads=[(fm_c, tt)], writes=[(fm_c, tt)])
                            p.op("dve", lambda e, tt=tt: e.tensor_tensor(
                                fr_c[:, tt:tt + 1], fm_c[:, tt:tt + 1], fm_c[:, tt:tt + 1],
                                ALU.mult),
                                reads=[(fm_c, tt)], writes=[(fr_c, tt)])
                            p.op("dve", lambda e, tt=tt: e.scalar_tensor_tensor(
                                fr_c[:, tt:tt + 1], fv_c[:, tt:tt + 1], 1.0 / D,
                                fr_c[:, tt:tt + 1], ALU.mult, ALU.subtract),
                                reads=[(fv_c, tt), (fr_c, tt)], writes=[(fr_c, tt)])
                        frk = [(fr_c, tt) for tt in range(4)]
                        p.op("dve", lambda e: e.tensor_scalar(fr2_c[:, :], fr_c[:, :],
                                                              1e-5, None, ALU.add),
                             reads=frk, writes=[fr2_c])
                        p.op("act", lambda e: e.activation(fr2_c[:, :], fr2_c[:, :], AF.Ln),
                             reads=[fr2_c], writes=[fr2_c])
                        p.op("act", lambda e: e.activation(fr2_c[:, :], fr2_c[:, :],
                                                           AF.Exp, scale=-0.5),
                             reads=[fr2_c], writes=[fr2_c])
                        p.op("dve", lambda e: e.tensor_tensor(ft_c[:, :], fr2_c[:, :],
                                                              fr2_c[:, :], ALU.mult),
                             reads=[fr2_c], writes=[ft_c])
                        p.op("dve", lambda e: e.tensor_tensor(ft_c[:, :], ft_c[:, :],
                                                              fr_c[:, :], ALU.mult),
                             reads=[ft_c] + frk, writes=[ft_c])
                        p.op("dve", lambda e: e.tensor_scalar(ft_c[:, :], ft_c[:, :],
                                                              1e-5, None, ALU.add),
                             reads=[ft_c], writes=[ft_c])
                        p.op("act", lambda e: e.activation(ft_c[:, :], ft_c[:, :], AF.Ln),
                             reads=[ft_c], writes=[ft_c])
                        p.op("act", lambda e: e.activation(ft_c[:, :], ft_c[:, :],
                                                           AF.Exp, scale=-0.5),
                             reads=[ft_c], writes=[ft_c])
                        p.op("dve", lambda e: e.tensor_tensor(fr2_c[:, :], fr2_c[:, :],
                                                              ft_c[:, :], ALU.mult),
                             reads=[fr2_c, ft_c], writes=[fr2_c])
                        for tt in range(4):
                            par = tt % 2
                            xt, xn, xq = xo_t[par], xnq[par], xqb[par]
                            gtile = half * 4 + tt
                            grow = base + tt * 128
                            p.dma("dsp", xt[:, :], xqin[grow:grow + 128, :], writes=[xt])
                            p.op("dve", lambda e, xt=xt, xn=xn, tt=tt: e.tensor_scalar(
                                xn[:, :], xt[:, :], fm_c[:, tt:tt + 1], fr2_c[:, tt:tt + 1],
                                ALU.subtract, ALU.mult),
                                reads=[xt, (fm_c, tt), fr2_c], writes=[xn])
                            p.op("dve", lambda e, xn=xn: e.tensor_tensor(
                                xn[:, :], xn[:, :], nwB[:, :], ALU.mult),
                                reads=[xn] + nwk, writes=[xn])
                            p.op("dve", lambda e, xn=xn: e.reduce_max(
                                am_c[:, 0:1], xn[:, :], AX.X, apply_absolute_value=True),
                                reads=[xn], writes=[(am_c, 0)])
                            p.op("dve", lambda e: e.tensor_scalar(
                                am_c[:, 0:1], am_c[:, 0:1], 1e-5, None, ALU.max),
                                reads=[(am_c, 0)], writes=[(am_c, 0)])
                            p.op("dve", lambda e, gtile=gtile: e.scalar_tensor_tensor(
                                c1cols[:, gtile:gtile + 1], am_c[:, 0:1], 1.0 / 127,
                                mgu_c[:, 0:1], ALU.mult, ALU.mult),
                                reads=[(am_c, 0), mgu_c], writes=[(c1cols, gtile)])
                            p.op("dve", lambda e: e.reciprocal(sc_c[:, 0:1], am_c[:, 0:1]),
                                 reads=[(am_c, 0)], writes=[sc_c])
                            p.op("dve", lambda e: e.tensor_scalar(
                                sc_c[:, 0:1], sc_c[:, 0:1], 127.0, None, ALU.mult),
                                reads=[sc_c], writes=[sc_c])
                            p.op("dve", lambda e, xn=xn: e.tensor_scalar(
                                xn[:, :], xn[:, :], sc_c[:, 0:1], None, ALU.mult),
                                reads=[xn, sc_c], writes=[xn])
                            p.op("dve", lambda e, xn=xn: e.tensor_copy(xqi[:, :], xn[:, :]),
                                 reads=[xn], writes=[xqi])
                            p.op("dve", lambda e, xq=xq: e.tensor_copy(xq[:, :], xqi[:, :]),
                                 reads=[xqi], writes=[xq])
                            p.dma("dact",
                                  xq_dram[half * 512 + tt * 128: half * 512 + (tt + 1) * 128, :],
                                  xq[:, :], reads=[xq], writes=[(xq_dram, gtile)])
            p.barrier()
            p.op("dve", lambda e: e.memset(am2c[:, :], 0.0),
                 writes=[(am2c, g) for g in range(8)])
            m4s = ExitStack()
            sbm = lambda name, shape, dt=F32: m4s.enter_context(
                nc.sbuf_tensor(name, shape, dt))
            if True:
                        xqfm = [sbm(f"xqfm{i}", [128, 16 * 512], BF16) for i in range(2)]
                        gW = [sbm(f"gW_{i}", [128, 16 * 512], BF16) for i in range(2)]
                        uW = [sbm(f"uW_{i}", [128, 16 * 512], BF16) for i in range(2)]
                        hidst = [sbm(f"hidst_{i}", [128, 512], BF16) for i in range(3)]
                        am_t = sbm("am_t", [128, 3])
                        for half in range(2):
                            for kc in range(16):
                                p.dma("dsp", xqfm[half][:, kc * 512:(kc + 1) * 512],
                                      xq_dram[half * 512:(half + 1) * 512,
                                              kc * 128:(kc + 1) * 128],
                                      reads=[(xq_dram, half * 4 + tt) for tt in range(4)],
                                      writes=[(xqfm[half], kc)], transpose=True)
                        for kp in range(11):
                            wpar = kp % 2
                            for kc in range(16):
                                p.dma("dsp", gW[wpar][:, kc * 512:(kc + 1) * 512],
                                      wTg[kc * 128:(kc + 1) * 128, kp * 512:(kp + 1) * 512],
                                      reads=[wTg], writes=[(gW[wpar], kc)])
                                p.dma("dsp", uW[wpar][:, kc * 512:(kc + 1) * 512],
                                      wTg[kc * 128:(kc + 1) * 128,
                                          FG + kp * 512: FG + (kp + 1) * 512],
                                      reads=[wTg], writes=[(uW[wpar], kc)])
                            for gtile in range(8):
                                half, tt = gtile // 4, gtile % 4
                                xf = xqfm[half]
                                pg = [psA, psC, psG][gtile % 3]
                                pu = [psB, psD, psH][gtile % 3]
                                xk = [(xf, kc) for kc in range(16)]
                                p.group("pe", [
                                    (lambda e, kc=kc, tt=tt, wpar=wpar, pg=pg, xf=xf: e.matmul(
                                        pg[:, 0:512],
                                        xf[:, kc * 512 + tt * 128: kc * 512 + (tt + 1) * 128],
                                        gW[wpar][:, kc * 512:(kc + 1) * 512],
                                        start=(kc == 0), stop=(kc == 15)),
                                     [(xf, kc), (gW[wpar], kc)], [pg])
                                    for kc in range(16)],
                                    reads=xk + [(gW[wpar], kc) for kc in range(16)],
                                    writes=[pg])
                                p.group("pe", [
                                    (lambda e, kc=kc, tt=tt, wpar=wpar, pu=pu, xf=xf: e.matmul(
                                        pu[:, 0:512],
                                        xf[:, kc * 512 + tt * 128: kc * 512 + (tt + 1) * 128],
                                        uW[wpar][:, kc * 512:(kc + 1) * 512],
                                        start=(kc == 0), stop=(kc == 15)),
                                     [(xf, kc), (uW[wpar], kc)], [pu])
                                    for kc in range(16)],
                                    reads=xk + [(uW[wpar], kc) for kc in range(16)],
                                    writes=[pu])
                                spar = gtile % 2
                                p.op("act", lambda e, spar=spar, gtile=gtile, pg=pg: e.activation(
                                    sgf2[spar][:, :], pg[:, 0:512], AF.Relu,
                                    scale=c1cols[:, gtile:gtile + 1]),
                                    reads=[pg, (c1cols, gtile)], writes=[sgf2[spar]])
                                p.op("act", lambda e, spar=spar: e.activation(
                                    sgf2[spar][:, :], sgf2[spar][:, :], AF.Square),
                                    reads=[sgf2[spar]], writes=[sgf2[spar]])
                                hst = hidst[gtile % 3]
                                p.op("dve", lambda e, spar=spar, gtile=gtile, hst=hst, pu=pu:
                                     e.scalar_tensor_tensor(
                                         hst[:, :],
                                         pu[:, 0:512], c1cols[:, gtile:gtile + 1],
                                         sgf2[spar][:, :], ALU.mult, ALU.mult),
                                     reads=[pu, (c1cols, gtile), sgf2[spar]],
                                     writes=[hst])
                                p.op("dve", lambda e, hst=hst, gtile=gtile: e.reduce_max(
                                    am_t[:, gtile % 3:gtile % 3 + 1], hst[:, :], AX.X,
                                    apply_absolute_value=True),
                                    reads=[hst], writes=[(am_t, gtile % 3)])
                                p.op("dve", lambda e, gtile=gtile: e.tensor_tensor(
                                    am2c[:, gtile:gtile + 1], am2c[:, gtile:gtile + 1],
                                    am_t[:, gtile % 3:gtile % 3 + 1], ALU.max),
                                    reads=[(am2c, gtile), (am_t, gtile % 3)],
                                    writes=[(am2c, gtile)])
                                p.dma("dact",
                                      hid_dram[gtile * 128:(gtile + 1) * 128,
                                               kp * 512:(kp + 1) * 512],
                                      hst[:, :], reads=[hst],
                                      writes=[(hid_dram, (half, tt, kp))])
                        m4s.close()
                        p.barrier()
                        q2s = ExitStack()
                        sbq2 = lambda name, shape, dt=F32: q2s.enter_context(
                            nc.sbuf_tensor(name, shape, dt))
                        hload = [sbq2(f"hload_{i}", [128, FG], BF16) for i in range(2)]
                        h2i = sbq2("h2i", [128, 512], I32)
                        for gtile in range(8):
                            half, tt = gtile // 4, gtile % 4
                            hl = hload[gtile % 2]
                            p.dma("dsp", hl[:, :],
                                  hid_dram[gtile * 128:(gtile + 1) * 128, :],
                                  reads=[(hid_dram, (half, tt, kp)) for kp in range(11)],
                                  writes=[hl])
                            p.op("dve", lambda e, gtile=gtile: e.tensor_scalar(
                                am_c[:, 0:1], am2c[:, gtile:gtile + 1], 1e-5, None,
                                ALU.max),
                                reads=[(am2c, gtile)], writes=[(am_c, 0)])
                            p.op("dve", lambda e, gtile=gtile: e.scalar_tensor_tensor(
                                c2cols2[:, gtile:gtile + 1], am_c[:, 0:1], 1.0 / 127,
                                mgu_c[:, 1:2], ALU.mult, ALU.mult),
                                reads=[(am_c, 0), mgu_c], writes=[(c2cols2, gtile)])
                            p.op("dve", lambda e: e.reciprocal(sc_c[:, 0:1], am_c[:, 0:1]),
                                 reads=[(am_c, 0)], writes=[sc_c])
                            p.op("dve", lambda e: e.tensor_scalar(
                                sc_c[:, 0:1], sc_c[:, 0:1], 127.0, None, ALU.mult),
                                reads=[sc_c], writes=[sc_c])
                            for kp in range(11):
                                hap = hl[:, kp * 512:(kp + 1) * 512]
                                p.op("dve", lambda e, hap=hap, kp=kp, hl=hl: e.tensor_scalar(
                                    h2i[:, :], hap, sc_c[:, 0:1], None, ALU.mult),
                                    reads=[(hl, kp), sc_c], writes=[h2i])
                                p.op("dve", lambda e, hap=hap, kp=kp, hl=hl: e.tensor_copy(
                                    hap, h2i[:, :]),
                                    reads=[h2i], writes=[(hl, kp)])
                            p.dma("dact",
                                  xq2_dram[gtile * 128:(gtile + 1) * 128, :],
                                  hl[:, :], reads=[(hl, kp) for kp in range(11)],
                                  writes=[(xq2_dram, gtile)])
                        q2s.close()
            p.barrier()
            h5 = ExitStack()
            sbh5 = lambda name, shape, dt=F32: h5.enter_context(
                nc.sbuf_tensor(name, shape, dt))
            with h5:
                xq2fm = sbh5("xq2fm", [128, 44 * 512], BF16)
                wdW = [sbh5(f"wdW_{i}", [128, 11 * 512], BF16) for i in range(2)]
                for half in range(2):
                    for kc in range(44):
                        p.dma("dsp", xq2fm[:, kc * 512:(kc + 1) * 512],
                              xq2_dram[half * 512:(half + 1) * 512,
                                       kc * 128:(kc + 1) * 128],
                              reads=[(xq2_dram, half * 4 + tt) for tt in range(4)],
                              writes=[(xq2fm, (half, kc))], transpose=True)
                    for dt4 in range(4):
                        for st4 in range(4):
                            wb = wdW[st4 % 2]
                            for kk in range(11):
                                kc = st4 * 11 + kk
                                p.dma("dsp", wb[:, kk * 512:(kk + 1) * 512],
                                      wTd[kc * 128:(kc + 1) * 128,
                                          dt4 * 512:(dt4 + 1) * 512],
                                      reads=[wTd], writes=[(wb, kk)])
                            for tt in range(4):
                                pX = [psA, psB, psC, psD][tt]
                                p.group("pe", [
                                    (lambda e, pX=pX, kc=st4 * 11 + kk, kk=kk, tt=tt, wb=wb:
                                     e.matmul(
                                         pX[:, 0:512],
                                         xq2fm[:, kc * 512 + tt * 128:
                                               kc * 512 + (tt + 1) * 128],
                                         wb[:, kk * 512:(kk + 1) * 512],
                                         start=(kc == 0), stop=(kc == 43)),
                                     [(xq2fm, (half, st4 * 11 + kk)), (wb, kk)], [pX])
                                    for kk in range(11)],
                                    reads=[(xq2fm, (half, st4 * 11 + kk))
                                           for kk in range(11)] +
                                          [(wb, kk) for kk in range(11)],
                                    writes=[pX])
                        for tt in range(4):
                            pX = [psA, psB, psC, psD][tt]
                            gtile = half * 4 + tt
                            spar = tt % 2
                            p.op("act", lambda e, pX=pX, spar=spar, gtile=gtile: e.activation(
                                sgf2[spar][:, :], pX[:, 0:512], AF.Sigmoid,
                                scale=c2cols2[:, gtile:gtile + 1]),
                                reads=[pX, (c2cols2, gtile)], writes=[sgf2[spar]])
                            p.op("dve", lambda e, pX=pX, spar=spar, gtile=gtile, dt4=dt4:
                                 e.scalar_tensor_tensor(
                                     g_sb[:, gtile * D + dt4 * 512: gtile * D + (dt4 + 1) * 512],
                                     pX[:, 0:512], c2cols2[:, gtile:gtile + 1],
                                     sgf2[spar][:, :], ALU.mult, ALU.mult),
                                 reads=[pX, (c2cols2, gtile), sgf2[spar]],
                                 writes=[(g_sb, (gtile, dt4))])

            p.barrier()
            f5 = ExitStack()
            sb5 = lambda name, shape, dt=F32: f5.enter_context(nc.sbuf_tensor(name, shape, dt))
            with f5:
                srm = sb5("srm", [128, 16 * 1024], BF16)
                wp_sb = [sb5(f"wp_sb{i}", [128, 16 * 512], BF16) for i in range(2)]
                out_sb = [sb5(f"out_sb{i}", [128, 512]) for i in range(2)]
                _qoff = []

                def _srm_load(e, kc):
                    if not _qoff:
                        _qoff.append((nc.partition_id() % 4) * 1024)
                    return e.dma_start(srm[:, kc * 1024:(kc + 1) * 1024],
                                       xch_r[kc * 128:(kc + 1) * 128,
                                             bass.ds(_qoff[0], 1024)])

                for kc in range(16):
                    p.dma_fn("dpool", (lambda e, kc=kc: _srm_load(e, kc)),
                             reads=[xch_r], writes=[(srm, kc)])
                for nh in range(4):
                    wp = wp_sb[nh % 2]
                    for kc in range(16):
                        p.dma("dpool", wp[:, kc * 512:(kc + 1) * 512],
                              wprojd[kc * 128:(kc + 1) * 128, nh * 512:(nh + 1) * 512],
                              reads=[], writes=[(wp, kc)])
                    for tt in range(8):
                        par = tt % 2
                        xt, ob_ = xo_t[par], out_sb[par]
                        p.dma("dsp", xt[:, :], xqin[tt * 128:(tt + 1) * 128, :], writes=[xt])
                        pX = [psA, psB][tt % 2]
                        ncol = nh * 512
                        p.group("pe", [
                            (lambda e, pX=pX, kc=kc, tt=tt, wp=wp: e.matmul(
                                pX[:, 0:512],
                                srm[:, kc * 1024 + tt * 128: kc * 1024 + (tt + 1) * 128],
                                wp[:, kc * 512:(kc + 1) * 512],
                                start=(kc == 0), stop=(kc == 15)),
                             [(srm, kc), (wp, kc)], [pX])
                            for kc in range(16)],
                            reads=[(srm, kc) for kc in range(16)] +
                                  [(wp, kc) for kc in range(16)],
                            writes=[pX])
                        p.op("dve", lambda e, pX=pX, tt=tt, ncol=ncol, ob_=ob_:
                             e.tensor_tensor(
                                 ob_[:, 0:512], pX[:, 0:512],
                                 g_sb[:, tt * D + ncol: tt * D + ncol + 512], ALU.mult),
                             reads=[pX, (g_sb, (tt, ncol // 512))], writes=[ob_])
                        p.op("dve", lambda e, ob_=ob_, xt=xt, ncol=ncol:
                             e.tensor_tensor(
                                 ob_[:, 0:512], ob_[:, 0:512],
                                 xt[:, ncol:ncol + 512], ALU.add),
                             reads=[ob_, xt], writes=[ob_])
                        p.dma("dpool", OUT[tt * 128:(tt + 1) * 128, nh * 512:(nh + 1) * 512],
                              ob_[:, 0:512], reads=[ob_], writes=[(OUT, (tt, nh))])

        for dn, dst, src in dump_outs:
            p.dma("dpool", dst[:, :], src[:, :], reads=[src], writes=[dst])
        p.finish_wait("sp")
        p.emit()

    return nc


def _p0_tiles(wgud, wdnd):
    gu_t = [wgud[i * 512:(i + 1) * 512, :].rearrange("(r p) c -> p r c", p=128)
            for i in range(22)]
    dn_t = [wdnd[i * 128:(i + 1) * 128, :] for i in range(16)]
    return ([("g", i, gu_t[i], 1024) for i in range(22)] +
            [("d", i, dn_t[i], 768) for i in range(16)])


def _p0_sbap(buf, kind, w):
    if kind == "g":
        return buf[:, 0:w].rearrange("p (r c) -> p r c", c=256)
    return buf[:, 0:w]


def _p0_abs_part(nc, p, tiles, lo, hi, p0a, racc, junk):
    for idx in range(lo, hi):
        kind, i, ap, w = tiles[idx]
        par = idx % 2
        p.dma("dpool", _p0_sbap(p0a[par], kind, w), ap, writes=[p0a[par]])
        p.op("act", lambda e, par=par, w=w, idx=idx: e.activation(
            junk[:, 0:w], p0a[par][:, 0:w], AF.Abs, accum_out=racc[:, idx:idx + 1]),
            reads=[p0a[par]], writes=[junk, (racc, idx)])


def _p0_reduce_a(nc, p, red_in, red_out, red_sb, red_sb2, invn_sb, meanrow,
                 thrrow, racc, ones_c, psD):
    p.op("pe", lambda e: e.matmul(psD[0:1, 64:104], ones_c[:, :], racc[:, 0:40],
                                  start=True, stop=True),
         reads=[ones_c] + [(racc, i) for i in range(38)], writes=[psD])
    p.op("dve", lambda e: e.reduce_sum(red_sb[0:1, 0:1], psD[0:1, 64:86], AX.X),
         reads=[psD], writes=[(red_sb, 0)])
    p.op("dve", lambda e: e.reduce_sum(red_sb[0:1, 1:2], psD[0:1, 86:102], AX.X),
         reads=[psD], writes=[(red_sb, 1)])
    p.op("dve", lambda e: e.memset(red_sb[0:1, 2:4], 0.0), writes=[(red_sb, 2)])
    p.dma("dpool", red_in[:, :], red_sb[:, :],
          reads=[(red_sb, 0), (red_sb, 1), (red_sb, 2)], writes=[red_in])
    p.cc("AllReduce", ALU.add, G8, [red_in[:, :]], [red_out[:, :]],
         reads=[red_in], writes=[red_out])
    p.dma("dpool", red_sb2[:, :], red_out[:, :], reads=[red_out], writes=[red_sb2])
    p.op("pool", lambda e: e.tensor_tensor(meanrow[:, :], red_sb2[:, :], invn_sb[:, :],
                                           ALU.mult),
         reads=[red_sb2, invn_sb], writes=[meanrow])
    p.op("pool", lambda e: e.tensor_scalar(meanrow[:, :], meanrow[:, :], 1e-5, None, ALU.max),
         reads=[meanrow], writes=[meanrow])
    p.op("pool", lambda e: e.tensor_scalar(thrrow[:, :], meanrow[:, :], 0.5, None, ALU.mult),
         reads=[meanrow], writes=[thrrow])


def _p0_reduce_b(nc, p, thrrow, thr4, nthr4, ones_r, psD):
    p.op("pe", lambda e: e.matmul(psD[:, 96:100], ones_r[:, :], thrrow[0:1, 0:4],
                                  start=True, stop=True),
         reads=[ones_r, thrrow], writes=[psD])
    p.op("dve", lambda e: e.tensor_copy(thr4[:, :], psD[:, 96:100]),
         reads=[psD], writes=[thr4])
    p.op("pool", lambda e: e.tensor_scalar(nthr4[:, :], thr4[:, :], -1.0, None, ALU.mult),
         reads=[thr4], writes=[nthr4])


def _p0_quant_part(nc, p, tiles, lo, hi, p0a, qjb, thr4, nthr4, wqn_g, wqn_d):
    for idx in range(lo, hi):
        kind, i, ap, w = tiles[idx]
        par = idx % 2
        tc = 0 if kind == "g" else 1
        p.dma("dpool", _p0_sbap(p0a[par], kind, w), ap, writes=[p0a[par]])
        for half in range(2):
            c0 = half * 512
            c1 = min(w, c0 + 512)
            sw = c1 - c0
            p.op("dve", lambda e, par=par, c0=c0, c1=c1, sw=sw, tc=tc: e.tensor_scalar(
                qjb[:, 0:sw], p0a[par][:, c0:c1], nthr4[:, tc:tc + 1], None, ALU.is_lt),
                reads=[p0a[par], nthr4], writes=[qjb])
            p.op("dve", lambda e, par=par, c0=c0, c1=c1, sw=sw, tc=tc:
                 e.scalar_tensor_tensor(
                     qjb[:, 0:sw], p0a[par][:, c0:c1], thr4[:, tc:tc + 1],
                     qjb[:, 0:sw], ALU.is_gt, ALU.subtract),
                 reads=[p0a[par], thr4, qjb], writes=[qjb])
            if kind == "g":
                dst = wqn_g[i * 512:(i + 1) * 512, :].rearrange(
                    "(r p) c -> p r c", p=128)[:, half * 2:half * 2 + 2, :]
                p.dma("dpool", dst, qjb[:, 0:512].rearrange("p (r c) -> p r c", c=256),
                      reads=[qjb], writes=[(wqn_g, (i, half))])
            else:
                p.dma("dpool", wqn_d[i * 128:(i + 1) * 128, c0:c1], qjb[:, 0:sw],
                      reads=[qjb], writes=[(wqn_d, (i, half))])


def _p0_transpose(nc, p, wqn_g, wqn_d, wTg_p, wTd_p, wTg, wTd, trbuf):
    gk = [(wqn_g, (i, h)) for i in range(22) for h in range(2)]
    dk = [(wqn_d, (i, h)) for i in range(16) for h in range(2)]
    for cc in range(2):
        for hf in range(4):
            p.dma("dsp", trbuf[:, 0:2816],
                  wqn_g[hf * 2816:(hf + 1) * 2816, cc * 128:(cc + 1) * 128],
                  reads=gk, writes=[trbuf], transpose=True)
            p.dma("dpool", wTg_p[cc * 128:(cc + 1) * 128, hf * 2816:(hf + 1) * 2816],
                  trbuf[:, 0:2816], reads=[trbuf], writes=[(wTg_p, (cc, hf))])
    for cc in range(6):
        p.dma("dsp", trbuf[:, 0:D], wqn_d[:, cc * 128:(cc + 1) * 128],
              reads=dk, writes=[trbuf], transpose=True)
        p.dma("dpool", wTd_p[cc * 128:(cc + 1) * 128, :], trbuf[:, 0:D],
              reads=[trbuf], writes=[(wTd_p, cc)])
    p.cc("AllGather", ALU.bypass, G8, [wTg_p[:, :]], [wTg[:, :]],
         reads=[(wTg_p, (cc, hf)) for cc in range(2) for hf in range(4)], writes=[wTg])
    p.cc("AllGather", ALU.bypass, G8, [wTd_p[:, :]], [wTd[:, :]],
         reads=[(wTd_p, cc) for cc in range(6)], writes=[wTd])


_NC_CACHE = []


def kernel(L0_attn_x, L0_attn_w0, L0_attn_w1, L0_attn_w2, L0_attn_W_qkv,
           L0_attn_W_lr, L0_attn_b_lr, L0_attn_W_proj, L0_attn_onorm_w,
           L0_ffn_w_gate_up, L0_ffn_w_down, L0_ffn_norm_weight):
    x = np.ascontiguousarray(np.asarray(L0_attn_x), dtype=np.float32)
    Wqkv = np.asarray(L0_attn_W_qkv, dtype=np.float32)
    Wlr = np.asarray(L0_attn_W_lr, dtype=np.float32)
    b_lr = np.asarray(L0_attn_b_lr, dtype=np.float32)
    w0 = np.asarray(L0_attn_w0, dtype=np.float32)
    w1 = np.asarray(L0_attn_w1, dtype=np.float32)
    w2 = np.asarray(L0_attn_w2, dtype=np.float32)
    onw = np.asarray(L0_attn_onorm_w, dtype=np.float32)
    Wproj = np.ascontiguousarray(np.asarray(L0_attn_W_proj, dtype=np.float32))
    wgu = np.asarray(L0_ffn_w_gate_up, dtype=np.float32)
    wdn = np.asarray(L0_ffn_w_down, dtype=np.float32)
    nw = np.asarray(L0_ffn_norm_weight, dtype=np.float32)

    in_maps = []
    for core in range(8):
        b, g = core // 4, core % 4
        heads = [4 * g + hh for hh in range(HPC)]
        qkv_cols = [j * D + h * HD + d for j in range(3) for h in heads for d in range(HD)]
        lr_cols = [j * 16 + h for j in range(3) for h in heads]
        wdn_s = np.zeros((D, DN_PAD), np.float32)
        lo = core * DN_PAD
        hi = min(lo + DN_PAD, FG)
        if hi > lo:
            wdn_s[:, 0:hi - lo] = wdn[:, lo:hi]
        in_maps.append({
            "x_b": np.ascontiguousarray(x[b]),
            "x_q": np.ascontiguousarray(x[b][g * 1024:(g + 1) * 1024]),
            "wqkv": np.ascontiguousarray(Wqkv[:, qkv_cols]),
            "wlr": np.ascontiguousarray(Wlr[:, lr_cols]),
            "blr": np.ascontiguousarray(np.tile(b_lr[lr_cols][None, :], (1, 4))),
            "w0": np.ascontiguousarray(w0[heads].reshape(HPC * HD, HD)),
            "w1": np.ascontiguousarray(w1[heads].reshape(HPC * HD, HD)),
            "w2": np.ascontiguousarray(w2[heads].reshape(HPC * HD, HD)),
            "onw": np.ascontiguousarray(onw[None, :]),
            "wproj": Wproj,
            "wgu": np.ascontiguousarray(wgu[:, core * 256:(core + 1) * 256]),
            "wdn": wdn_s,
            "nw": np.ascontiguousarray(nw[None, :]),
        })

    if not _NC_CACHE:
        _NC_CACHE.append(build_program())
    nc = _NC_CACHE[0]
    res = run_bass_kernel_spmd(nc, in_maps, core_ids=list(range(8)))
    out = np.empty((2, L, D), np.float32)
    for core in range(8):
        b, g = core // 4, core % 4
        out[b, g * 1024:(g + 1) * 1024, :] = res.results[core]["out"]
    return out


# version 23
# speedup vs baseline: 1.0350x; 1.0350x over previous
import numpy as np
import ml_dtypes
import concourse.bass as bass
import concourse.mybir as mybir
from concourse.bass_utils import run_bass_kernel_spmd
from contextlib import ExitStack

F32 = mybir.dt.float32
BF16 = mybir.dt.bfloat16
I32 = mybir.dt.int32
AF = mybir.ActivationFunctionType
ALU = mybir.AluOpType
AX = mybir.AxisListType
BF = ml_dtypes.bfloat16

D = 2048
L = 4096
HD = 128
NCH = 8
HPC = 4
NS_A, NS_B, NS_C = 3.4445, -4.775, 2.0315
BASE_LR_INV = float(np.log(np.expm1(0.01)))
FG = 5632
GU_ROWS = 2 * FG
DN_PAD = 768


class Prog:

    ENGSEC = {"pe": "tensor", "act": "scalar", "dve": "vector", "pool": "gpsimd", "sp": "sync"}
    DMAQ = {"dsp": "sp", "dact": "act", "dpool": "pool", "dcc": "pool"}

    NLANES = 16

    def __init__(self, nc, stack, same_engine_sync=True):
        self.nc = nc
        self.same_engine_sync = same_engine_sync
        self.sems = {}
        self.counts = {}
        self.watermark = {}
        self.writers = {}
        self.readers = {}
        self.qidx = {}
        self.queue = {e: [] for e in self.ENGSEC}
        for name in self.ENGSEC:
            self.sems[name] = stack.enter_context(nc.semaphore(f"sem_{name}"))
            self.counts[name] = 0
        for name in self.DMAQ:
            self.qidx[name] = 0
            if name == "dcc":
                self.sems[name] = stack.enter_context(nc.semaphore(f"sem_{name}"))
                self.counts[name] = 0
            else:
                for i in range(self.NLANES):
                    ln = f"{name}_{i}"
                    self.sems[ln] = stack.enter_context(nc.semaphore(f"sem_{ln}"))
                    self.counts[ln] = 0

    def _key(self, b):
        if isinstance(b, tuple):
            t, k = b
        else:
            t, k = b, None
        return getattr(t, "name", None) or str(t), k

    def _dep_edges(self, reads, writes):
        edges = {}
        for t, key in reads:
            w = self.writers.get(t, {})
            items = list(w.items()) if key is None else [(k, v) for k, v in w.items() if k == key or k is None]
            for _, (s, c) in items:
                edges[s] = max(edges.get(s, 0), c)
        for t, key in writes:
            w = self.writers.get(t, {})
            items = list(w.items()) if key is None else [(k, v) for k, v in w.items() if k == key or k is None]
            for _, (s, c) in items:
                edges[s] = max(edges.get(s, 0), c)
            r = self.readers.get(t, {})
            ritems = (
                [x for lst in r.values() for x in lst]
                if key is None
                else [x for k, lst in r.items() if k == key or k is None for x in lst]
            )
            for s, c in ritems:
                edges[s] = max(edges.get(s, 0), c)
        return edges

    def _waits_for(self, engine, edges):
        waits = []
        for s, c in sorted(edges.items()):
            if s == engine and (engine == "pe" or not self.same_engine_sync):
                continue
            wm = self.watermark.get((engine, s), 0)
            if c <= wm:
                continue
            waits.append((s, c))
            self.watermark[(engine, s)] = c
        return waits

    def _record(self, sem_name, reads, writes):
        cnt = self.counts[sem_name]
        for t, key in reads:
            self.readers.setdefault(t, {}).setdefault(key, []).append((sem_name, cnt))
        for t, key in writes:
            d = self.writers.setdefault(t, {})
            if key is None:
                d.clear()
                self.readers.pop(t, None)
            d[key] = (sem_name, cnt)
            r = self.readers.get(t, {})
            if key in r:
                r[key] = []

    def op(self, engine, fn, reads=(), writes=(), inc=True):
        reads = [self._key(b) for b in reads]
        writes = [self._key(b) for b in writes]
        waits = self._waits_for(engine, self._dep_edges(reads, writes))
        if inc:
            self.counts[engine] += 1
        self.queue[engine].append((waits, fn, engine if inc else None, 1))
        self._record(engine, reads, writes)

    def group(self, engine, items, reads=(), writes=()):
        greads = [self._key(b) for b in reads]
        gwrites = [self._key(b) for b in writes]
        n = len(items)
        for i, (fn, r, w) in enumerate(items):
            r = [self._key(b) for b in r]
            w = [self._key(b) for b in w]
            waits = self._waits_for(engine, self._dep_edges(r, w))
            last = i == n - 1
            if last:
                self.counts[engine] += 1
            self.queue[engine].append((waits, fn, engine if last else None, 1))
        self._record(engine, greads, gwrites)

    def dma(self, queue, out_ap, in_ap, reads=(), writes=(), transpose=False, **kw):
        engine = self.DMAQ[queue]
        lane = queue if queue == "dcc" else f"{queue}_{self.qidx[queue] % self.NLANES}"
        self.qidx[queue] = self.qidx.get(queue, 0) + 1
        reads = [self._key(b) for b in reads]
        writes = [self._key(b) for b in writes]
        waits = self._waits_for(engine, self._dep_edges(reads, writes))
        self.counts[lane] += 16
        if transpose:
            fn = lambda e, o=out_ap, i=in_ap: e.dma_start_transpose(o, i)
        else:
            fn = lambda e, o=out_ap, i=in_ap, kw=kw: e.dma_start(o, i, **kw)
        self.queue[engine].append((waits, fn, lane, 16))
        self._record(lane, reads, writes)

    def dma_fn(self, queue, fn, reads=(), writes=()):
        engine = self.DMAQ[queue]
        lane = queue if queue == "dcc" else f"{queue}_{self.qidx[queue] % self.NLANES}"
        self.qidx[queue] = self.qidx.get(queue, 0) + 1
        reads = [self._key(b) for b in reads]
        writes = [self._key(b) for b in writes]
        waits = self._waits_for(engine, self._dep_edges(reads, writes))
        self.counts[lane] += 16
        self.queue[engine].append((waits, fn, lane, 16))
        self._record(lane, reads, writes)

    def cc(self, kind, op, groups, ins_aps, outs_aps, reads=(), writes=()):
        reads = [self._key(b) for b in reads]
        writes = [self._key(b) for b in writes]
        waits = self._waits_for("pool", self._dep_edges(reads, writes))
        self.counts["dcc"] += 1
        fn = lambda e: e.collective_compute(kind, op, replica_groups=groups,
                                            ins=ins_aps, outs=outs_aps)
        self.queue["pool"].append((waits, fn, "dcc", 1))
        self._record("dcc", reads, writes)

    def barrier(self, exclude=("dcc",)):
        for e in self.ENGSEC:
            edges = {s: c for s, c in self.counts.items()
                     if c > 0 and s != e and s not in exclude}
            waits = self._waits_for(e, edges)
            if waits:
                self.queue[e].append((waits, None, None, 0))

    def finish_wait(self, engine):
        edges = {s: c for s, c in self.counts.items() if c > 0 and s != engine}
        waits = self._waits_for(engine, edges)
        self.queue[engine].append((waits, None, None, 0))

    def emit(self):
        with self.nc.Block() as blk:
            for engine, ops in self.queue.items():
                if not ops:
                    continue
                sec = getattr(blk, self.ENGSEC[engine])

                def make(ops=ops):
                    def body(e):
                        for waits, fn, inc_sem, inc_val in ops:
                            for s, c in waits:
                                e.wait_ge(self.sems[s], c)
                            if fn is not None:
                                inst = fn(e)
                                if inc_sem is not None:
                                    inst.then_inc(self.sems[inc_sem], inc_val)
                    return body

                sec(make())


G8 = [list(range(8))]
G44 = [[0, 1, 2, 3], [4, 5, 6, 7]]


def build_program(dumps=()):
    nc = bass.Bass("TRN2", target_bir_lowering=False)

    xin = nc.dram_tensor("x_b", [L, D], F32, kind="ExternalInput")
    xqin = nc.dram_tensor("x_q", [1024, D], F32, kind="ExternalInput")
    wqkv = nc.dram_tensor("wqkv", [D, 12 * HD], F32, kind="ExternalInput")
    wlrd = nc.dram_tensor("wlr", [D, 12], F32, kind="ExternalInput")
    blrd = nc.dram_tensor("blr", [1, 48], F32, kind="ExternalInput")
    w0d = nc.dram_tensor("w0", [HPC * HD, HD], F32, kind="ExternalInput")
    w1d = nc.dram_tensor("w1", [HPC * HD, HD], F32, kind="ExternalInput")
    w2d = nc.dram_tensor("w2", [HPC * HD, HD], F32, kind="ExternalInput")
    onwd = nc.dram_tensor("onw", [1, HD], F32, kind="ExternalInput")
    wprojd = nc.dram_tensor("wproj", [D, D], F32, kind="ExternalInput")
    wgud = nc.dram_tensor("wgu", [GU_ROWS, 256], F32, kind="ExternalInput")
    wdnd = nc.dram_tensor("wdn", [D, DN_PAD], F32, kind="ExternalInput")
    nwd = nc.dram_tensor("nw", [1, D], F32, kind="ExternalInput")
    OUT = nc.dram_tensor("out", [1024, D], F32, kind="ExternalOutput")

    ident_b = nc.inline_tensor(np.eye(HD, dtype=BF), name="ident_b")
    aI_c = nc.inline_tensor((NS_A * np.eye(HD)).astype(np.float32), name="aI_c")
    ones_col = nc.inline_tensor(np.ones((HD, 1), np.float32), name="ones_col")
    ones_rowc = nc.inline_tensor(np.ones((1, HD), np.float32), name="ones_rowc")
    invn = nc.inline_tensor(np.array([[1.0 / (GU_ROWS * D), 1.0 / (D * FG), 1.0, 1.0]],
                                     np.float32), name="invn")

    xn_dram = nc.dram_tensor("xn_dram", [L, D], BF16)
    qkv_dram = nc.dram_tensor("qkv_dram", [12 * HD, L], BF16)
    o_dram = nc.dram_tensor("o_dram", [HPC * HD, L], BF16)
    xch_a2a = nc.dram_tensor("xch_a2a", [512, L], BF16)
    xch_r = nc.dram_tensor("xch_r", [4 * 512, L], BF16)
    wqn_g = nc.dram_tensor("wqn_g", [GU_ROWS, 256], BF16)
    wqn_d = nc.dram_tensor("wqn_d", [D, DN_PAD], BF16)
    wTg_p = nc.dram_tensor("wTg_p", [256, GU_ROWS], BF16)
    wTd_p = nc.dram_tensor("wTd_p", [DN_PAD, D], BF16)
    wTg = nc.dram_tensor("wTg", [D, GU_ROWS], BF16)
    wTd = nc.dram_tensor("wTd", [8 * DN_PAD, D], BF16)
    red_in = nc.dram_tensor("red_in", [1, 4], F32)
    red_out = nc.dram_tensor("red_out", [1, 4], F32)
    xq_dram = nc.dram_tensor("xq_dram", [1024, D], BF16)
    xq2_dram = nc.dram_tensor("xq2_dram", [1024, FG], BF16)
    ss_dram = nc.dram_tensor("ss_dram", [8, L], BF16)
    hid_dram = nc.dram_tensor("hid_dram", [1024, FG], BF16)

    dump_outs = []
    scr = {"xn_dram": xn_dram, "qkv_dram": qkv_dram, "o_dram": o_dram,
           "xch_a2a": xch_a2a, "xch_r": xch_r, "wTg": wTg, "wTd": wTd,
           "xq_dram": xq_dram, "xq2_dram": xq2_dram, "red_out": red_out,
           "hid_dram": hid_dram, "ss_dram": ss_dram}
    for dn in dumps:
        src = scr[dn]
        dst = nc.dram_tensor("dump_" + dn, list(src.shape), src.dtype, kind="ExternalOutput")
        dump_outs.append((dn, dst, src))

    stack = ExitStack()
    with stack:
        p = Prog(nc, stack)
        sb = lambda name, shape, dt=F32: stack.enter_context(nc.sbuf_tensor(name, shape, dt))
        ps = lambda name, shape, dt=F32: stack.enter_context(nc.psum_tensor(name, shape, dt))

        psA = ps("psA", [128, 512])
        psB = ps("psB", [128, 512])
        psC = ps("psC", [128, 512])
        psD = ps("psD", [128, 512])
        psG = ps("psG", [128, 512])
        psH = ps("psH", [128, 512])
        psI = ps("psI", [128, 512])
        psT = ps("psT", [128, 128], BF16)
        psm = [psG, psH, psI]

        identb = sb("identb", [128, 128], BF16)
        aI = sb("aI", [128, 128])
        ones_c = sb("ones_c", [128, 1])
        ones_r = sb("ones_r", [1, 128])
        ones_rb = sb("ones_rb", [1, 128], BF16)
        onw_r = sb("onw_r", [1, 128])
        onw_c = sb("onw_c", [128, 1])
        blrB = sb("blrB", [128, 48])
        blr_sb = sb("blr_sb", [1, 48])
        invn_sb = sb("invn_sb", [1, 4])
        wf = sb("wf", [128, 12 * 128])
        wbf = sb("wbf", [128, 12 * 128], BF16)
        w1T = sb("w1T", [128, 4 * 128], BF16)
        wn_row = sb("wn_row", [1, 12 * 128])
        wsq = sb("wsq", [128, 12 * 128])
        ssw_row = sb("ssw_row", [1, 12 * 128])
        lr_sb = sb("lr_sb", [128, 32 * 12])
        zrow = sb("zrow", [128, 32 * 12])
        zt1 = sb("zt1", [128, 32 * 12])
        ss_st = sb("ss_st", [1, 2048], BF16)
        n2acc = sb("n2acc", [128, 16])
        n2row = sb("n2row", [1, 16])
        cfrow = sb("cfrow", [1, 16])
        c2row = sb("c2row", [1, 16])
        cfcols = sb("cfcols", [128, 16])
        c2cols = sb("c2cols", [128, 16])
        meanrow = sb("meanrow", [1, 4])
        thrrow = sb("thrrow", [1, 4])
        thr4 = sb("thr4", [128, 4])
        nthr4 = sb("nthr4", [128, 4])
        red_sb = sb("red_sb", [1, 4])
        red_sb2 = sb("red_sb2", [1, 4])
        racc = sb("racc", [128, 40])
        junk = sb("junk", [128, 2048])

        p.dma("dsp", identb[:, :], ident_b[:, :], writes=[identb])
        p.dma("dsp", aI[:, :], aI_c[:, :], writes=[aI])
        p.dma("dsp", ones_c[:, :], ones_col[:, :], writes=[ones_c])
        p.dma("dsp", ones_r[:, :], ones_rowc[:, :], writes=[ones_r])
        p.dma("dpool", ones_rb[:, :], ones_rowc[:, :], writes=[ones_rb])
        p.dma("dsp", onw_r[:, :], onwd[:, :], writes=[onw_r])
        p.dma("dsp", blr_sb[:, :], blrd[:, :], writes=[blr_sb])
        p.dma("dsp", invn_sb[:, :], invn[:, :], writes=[invn_sb])
        p.op("pe", lambda e: e.matmul(psG[:, 0:1], onw_r[:, :], ones_r[0:1, 0:1],
                                      start=True, stop=True),
             reads=[onw_r, ones_r], writes=[psG])
        p.op("dve", lambda e: e.tensor_copy(onw_c[:, :], psG[:, 0:1]), reads=[psG], writes=[onw_c])
        p.op("pe", lambda e: e.matmul(psD[:, 0:48], ones_r[:, :], blr_sb[:, :],
                                      start=True, stop=True),
             reads=[ones_r, blr_sb], writes=[psD])
        p.op("dve", lambda e: e.tensor_scalar(blrB[:, :], psD[:, 0:48], BASE_LR_INV, None, ALU.add),
             reads=[psD], writes=[blrB])
        wfk = [(wf, (hh, j)) for hh in range(HPC) for j in range(3)]
        for hh in range(HPC):
            for j, wsrc in ((0, w0d), (1, w1d), (2, w2d)):
                col = (hh * 3 + j) * 128
                p.dma("dsp", wf[:, col:col + 128], wsrc[hh * 128:(hh + 1) * 128, :],
                      writes=[(wf, (hh, j))])
        p.op("dve", lambda e: e.tensor_copy(wbf[:, :], wf[:, :]), reads=wfk, writes=[wbf])
        for hh in range(HPC):
            colw = (hh * 3 + 1) * 128
            p.op("pe", lambda e, cw=colw: e.transpose(psT[:, :], wbf[:, cw:cw + 128], identb[:, :]),
                 reads=[wbf, identb], writes=[psT])
            p.op("dve", lambda e, hh=hh: e.tensor_copy(w1T[:, hh * 128:(hh + 1) * 128], psT[:, :]),
                 reads=[psT], writes=[(w1T, hh)])
        p.op("act", lambda e: e.activation(wsq[:, :], wf[:, :], AF.Square),
             reads=wfk, writes=[wsq])
        for kb in range(3):
            p.op("pe", lambda e, k=kb: e.matmul(psD[0:1, 0:512], ones_c[:, :],
                                                wsq[:, k * 512:(k + 1) * 512],
                                                start=True, stop=True),
                 reads=[ones_c, wsq], writes=[psD])
            p.op("dve", lambda e, k=kb: e.tensor_copy(ssw_row[0:1, k * 512:(k + 1) * 512],
                                                      psD[0:1, 0:512]),
                 reads=[psD], writes=[(ssw_row, kb)])
        sswk = [(ssw_row, kk) for kk in range(3)]
        p.op("act", lambda e: e.activation(wn_row[:, :], ssw_row[:, :], AF.Ln),
             reads=sswk, writes=[wn_row])
        p.op("act", lambda e: e.activation(wn_row[:, :], wn_row[:, :], AF.Exp, scale=0.5),
             reads=[wn_row], writes=[wn_row])

        mid1 = ExitStack()
        sb1 = lambda name, shape, dt=F32: mid1.enter_context(nc.sbuf_tensor(name, shape, dt))
        with mid1:
            wq_sb = sb1("wq_sb", [128, 16 * 1536], BF16)
            wlr_sb = sb1("wlr_sb", [128, 16 * 12], BF16)
            x_t = [sb1(f"x_t{i}", [128, D]) for i in range(4)]
            xn_b = [sb1(f"xn_b{i}", [128, D], BF16) for i in range(3)]
            xnfm = [sb1(f"xnfm{i}", [128, 16 * 512], BF16) for i in range(2)]
            sig_s = [sb1(f"sig_s{i}", [128, 512]) for i in range(2)]
            qk_s = [sb1(f"qk_s{i}", [128, 512]) for i in range(2)]
            qkb_s = [sb1(f"qkb_s{i}", [128, 512], BF16) for i in range(3)]
            p0a = [sb1(f"p0a{i}", [128, 1024]) for i in range(2)]
            qjb = sb1("qjb", [128, 512], BF16)
            sq2_b = [sb1(f"sq2_b{i}", [128, 512]) for i in range(2)]
            m_c = sb1("m_c", [128, 4])
            v_c = sb1("v_c", [128, 4])
            r_c = sb1("r_c", [128, 4])
            rr_c = sb1("rr_c", [128, 4])

            for kc in range(16):
                p.dma("dpool", wq_sb[:, kc * 1536:(kc + 1) * 1536],
                      wqkv[kc * 128:(kc + 1) * 128, :], writes=[(wq_sb, kc)])
                p.dma("dpool", wlr_sb[:, kc * 12:(kc + 1) * 12],
                      wlrd[kc * 128:(kc + 1) * 128, :], writes=[(wlr_sb, kc)])

            def ln_loads(s):
                for t4 in range(4):
                    xt = x_t[t4]
                    row0 = s * 512 + t4 * 128
                    p.dma(("dsp", "dact")[t4 % 2], xt[:, :], xin[row0:row0 + 128, :],
                          writes=[xt])

            def ln_stats(s, t4):
                    xt = x_t[t4]
                    p.op("act", lambda e, xt=xt, t4=t4: e.activation(
                        junk[:, :], xt[:, :], AF.Square, accum_out=v_c[:, t4:t4 + 1]),
                        reads=[xt], writes=[junk, (v_c, t4)])
                    p.op("dve", lambda e, xt=xt, t4=t4: e.reduce_sum(
                        m_c[:, t4:t4 + 1], xt[:, :], AX.X),
                        reads=[xt], writes=[(m_c, t4)])
                    p.op("dve", lambda e, t4=t4: e.tensor_scalar(
                        m_c[:, t4:t4 + 1], m_c[:, t4:t4 + 1], 1.0 / D, None, ALU.mult),
                        reads=[(m_c, t4)], writes=[(m_c, t4)])
                    p.op("dve", lambda e, t4=t4: e.tensor_tensor(
                        r_c[:, t4:t4 + 1], m_c[:, t4:t4 + 1], m_c[:, t4:t4 + 1], ALU.mult),
                        reads=[(m_c, t4)], writes=[(r_c, t4)])
                    p.op("dve", lambda e, t4=t4: e.scalar_tensor_tensor(
                        r_c[:, t4:t4 + 1], v_c[:, t4:t4 + 1], 1.0 / D, r_c[:, t4:t4 + 1],
                        ALU.mult, ALU.subtract),
                        reads=[(v_c, t4), (r_c, t4)], writes=[(r_c, t4)])

            def ln_finish(s):
                rck = [(r_c, t4) for t4 in range(4)]
                p.op("dve", lambda e: e.tensor_scalar(rr_c[:, :], r_c[:, :], 1e-5, None, ALU.add),
                     reads=rck, writes=[rr_c])
                p.op("act", lambda e: e.activation(rr_c[:, :], rr_c[:, :], AF.Ln),
                     reads=[rr_c], writes=[rr_c])
                p.op("act", lambda e: e.activation(rr_c[:, :], rr_c[:, :], AF.Exp, scale=-0.5),
                     reads=[rr_c], writes=[rr_c])
                spar = s % 2
                for t4 in range(4):
                    xt, xnb = x_t[t4], xn_b[t4 % 3]
                    row0 = s * 512 + t4 * 128
                    p.op("dve", lambda e, xt=xt, xnb=xnb, t4=t4: e.tensor_scalar(
                        xnb[:, :], xt[:, :], m_c[:, t4:t4 + 1], rr_c[:, t4:t4 + 1],
                        ALU.subtract, ALU.mult),
                        reads=[xt, (m_c, t4), rr_c], writes=[xnb])
                    p.dma(("dact", "dsp")[t4 % 2], xn_dram[row0:row0 + 128, :],
                          xnb[:, :], reads=[xnb], writes=[(xn_dram, (s, t4))])
                for kc in range(16):
                    p.dma("dsp", xnfm[spar][:, kc * 512:(kc + 1) * 512],
                          xn_dram[s * 512:(s + 1) * 512, kc * 128:(kc + 1) * 128],
                          reads=[(xn_dram, (s, t4)) for t4 in range(4)],
                          writes=[(xnfm[spar], (kc, t4)) for t4 in range(4)],
                          transpose=True)

            def emit_qkv(s, nxt, pwork=()):
                spar = s % 2
                pend = []
                pw = list(pwork)
                if nxt is not None:
                    ln_loads(nxt)
                for j in range(12):
                    if nxt is not None and j in (2, 4, 6, 8):
                        ln_stats(nxt, j // 2 - 1)
                    if nxt is not None and j == 10:
                        ln_finish(nxt)
                    if pw:
                        f = pw.pop(0)
                        if f:
                            f()
                    pj = [psA, psB, psC, psG, psH, psI][j % 6]
                    p.group("pe", [
                        (lambda e, pj=pj, j=j, kc=kc, spar=spar: e.matmul(
                            pj[:, 0:512],
                            wq_sb[:, kc * 1536 + j * 128: kc * 1536 + (j + 1) * 128],
                            xnfm[spar][:, kc * 512:(kc + 1) * 512],
                            start=(kc == 0), stop=(kc == 15)),
                         [(wq_sb, kc)] + [(xnfm[spar], (kc, t4)) for t4 in range(4)],
                         [pj])
                        for kc in range(16)],
                        reads=[(wq_sb, kc) for kc in range(16)] +
                              [(xnfm[spar], (kc, t4)) for kc in range(16)
                               for t4 in range(4)],
                        writes=[pj])
                    epar = j % 2
                    qpar = j % 3
                    p.op("act", lambda e, pj=pj, epar=epar: e.activation(
                        sig_s[epar][:, :], pj[:, 0:512], AF.Sigmoid),
                        reads=[pj], writes=[sig_s[epar]])
                    if j >= 8:
                        p.op("dve", lambda e, pj=pj, epar=epar, qpar=qpar: e.tensor_tensor(
                            qkb_s[qpar][:, :], pj[:, 0:512], sig_s[epar][:, :], ALU.mult),
                            reads=[pj, sig_s[epar]], writes=[qkb_s[qpar]])
                    else:
                        p.op("dve", lambda e, pj=pj, epar=epar: e.tensor_tensor(
                            qk_s[epar][:, :], pj[:, 0:512], sig_s[epar][:, :], ALU.mult),
                            reads=[pj, sig_s[epar]], writes=[qk_s[epar]])
                        p.op("dve", lambda e, epar=epar: e.tensor_tensor(
                            sq2_b[epar][:, :], qk_s[epar][:, :], qk_s[epar][:, :],
                            ALU.mult),
                            reads=[qk_s[epar]], writes=[sq2_b[epar]])
                        p.op("dve", lambda e, epar=epar, qpar=qpar: e.tensor_copy(
                            qkb_s[qpar][:, :], qk_s[epar][:, :]),
                            reads=[qk_s[epar]], writes=[qkb_s[qpar]])
                        pend.append((j, epar, qpar))
                    p.dma("dact", qkv_dram[j * 128:(j + 1) * 128, s * 512:(s + 1) * 512],
                          qkb_s[qpar][:, :], reads=[qkb_s[qpar]],
                          writes=[(qkv_dram, (j, s))])
                    while pend and (pend[0][0] < j or j == 11):
                        jd, ed, qd = pend.pop(0)
                        p.op("pe", lambda e, ed=ed: e.matmul(
                            psD[0:1, 0:512], ones_c[:, :], sq2_b[ed][:, :],
                            start=True, stop=True),
                            reads=[ones_c, sq2_b[ed]], writes=[psD])
                        p.op("dve", lambda e, qd=qd: e.tensor_copy(
                            ss_st[0:1, qd * 512:(qd + 1) * 512], psD[0:1, 0:512]),
                            reads=[psD], writes=[(ss_st, qd)])
                        p.dma("dact", ss_dram[jd:jd + 1, s * 512:(s + 1) * 512],
                              ss_st[0:1, qd * 512:(qd + 1) * 512],
                              reads=[(ss_st, qd)], writes=[(ss_dram, (jd, s))])
                for f in pw:
                    if f:
                        f()
                for tt in range(4):
                    p.group("pe", [
                        (lambda e, tt=tt, kc=kc, spar=spar: e.matmul(
                            psD[:, 448 + tt * 12: 448 + (tt + 1) * 12],
                            xnfm[spar][:, kc * 512 + tt * 128: kc * 512 + (tt + 1) * 128],
                            wlr_sb[:, kc * 12:(kc + 1) * 12],
                            start=(kc == 0), stop=(kc == 15)),
                         [(xnfm[spar], (kc, tt)), (wlr_sb, kc)], [psD])
                        for kc in range(16)],
                        reads=[(xnfm[spar], (kc, tt)) for kc in range(16)] +
                              [(wlr_sb, kc) for kc in range(16)],
                        writes=[psD])
                p.op("dve", lambda e, s=s: e.tensor_tensor(
                    zrow[:, s * 48:(s + 1) * 48], psD[:, 448:496], blrB[:, :], ALU.add),
                    reads=[psD, blrB], writes=[(zrow, s)])

            p0_tiles = _p0_tiles(wgud, wdnd)

            def abs_one(idx):
                return lambda: _p0_abs_part(nc, p, p0_tiles, idx, idx + 1,
                                            p0a, racc, junk)

            def quant_one(idx):
                return lambda: _p0_quant_part(nc, p, p0_tiles, idx, idx + 1,
                                              p0a, qjb, thr4, nthr4, wqn_g, wqn_d)

            ln_loads(0)
            for t4 in range(4):
                ln_stats(0, t4)
            ln_finish(0)
            for s in range(8):
                if s < 4:
                    pwk = [abs_one(i) for i in range((0, 10, 20, 30)[s],
                                                     (10, 20, 30, 38)[s])]
                    if s == 3:
                        pwk.append(lambda: _p0_reduce_a(
                            nc, p, red_in, red_out, red_sb, red_sb2, invn_sb,
                            meanrow, thrrow, racc, ones_c, psD))
                elif s == 4:
                    pwk = [None] * 6 + [lambda: _p0_reduce_b(
                        nc, p, thrrow, thr4, nthr4, ones_r, psD)]
                else:
                    pwk = [quant_one(i) for i in range((0, 13, 26)[s - 5],
                                                       (13, 26, 38)[s - 5])]
                emit_qkv(s, s + 1 if s + 1 < 8 else None, pwk)
            zread = [(zrow, s) for s in range(8)]
            p.op("act", lambda e: e.activation(zt1[:, :], zrow[:, :], AF.Abs),
                 reads=zread, writes=[zt1])
            p.op("act", lambda e: e.activation(zt1[:, :], zt1[:, :], AF.Exp, scale=-1.0),
                 reads=[zt1], writes=[zt1])
            p.op("act", lambda e: e.activation(zt1[:, :], zt1[:, :], AF.Ln, bias=1.0),
                 reads=[zt1], writes=[zt1])
            p.op("act", lambda e: e.activation(lr_sb[:, :], zrow[:, :], AF.Relu),
                 reads=zread, writes=[lr_sb])
            p.op("dve", lambda e: e.tensor_tensor(lr_sb[:, :], lr_sb[:, :], zt1[:, :], ALU.add),
                 reads=[lr_sb, zt1], writes=[lr_sb])

        p.barrier()

        mid2 = ExitStack()
        sb2 = lambda name, shape, dt=F32: mid2.enter_context(nc.sbuf_tensor(name, shape, dt))
        with mid2:
            qc_b = [sb2(f"qc_b{i}", [128, HPC * 512], BF16) for i in range(2)]
            kc_b = [sb2(f"kc_b{i}", [128, HPC * 512], BF16) for i in range(2)]
            vc_b = [sb2(f"vc_b{i}", [128, HPC * 512], BF16) for i in range(2)]
            ktvt = [sb2(f"ktvt{i}", [128, 4 * 1024], BF16) for i in range(2)]
            gate_f = [sb2(f"gate_f{i}", [128, 512]) for i in range(2)]
            s_f = [sb2(f"s_f{i}", [128, 512]) for i in range(2)]
            hb_f = [sb2(f"hb_f{i}", [128, 512]) for i in range(2)]
            dh_f = [sb2(f"dh_f{i}", [128, 512]) for i in range(2)]
            sg_f = [sb2(f"sg_f{i}", [128, 512]) for i in range(2)]
            t1_f = [sb2(f"t1_f{i}", [128, 512]) for i in range(2)]
            hidl_b = [sb2(f"hidl_b{i}", [128, 512], BF16) for i in range(4)]
            k0t_b = [sb2(f"k0t_b{i}", [128, 512], BF16) for i in range(4)]
            k2t_b = [sb2(f"k2t_b{i}", [128, 512], BF16) for i in range(4)]
            dgba_b = [sb2(f"dgba_b{i}", [128, 512], BF16) for i in range(4)]
            dhbm_b = [sb2(f"dhbm_b{i}", [128, 512], BF16) for i in range(4)]
            Gb = sb2("Gb", [128, 12 * 128], BF16)
            GTb = sb2("GTb", [128, 12 * 128], BF16)
            ns_scr = [sb2(f"ns_scr{i}", [128, 2 * 128]) for i in range(3)]
            ns_bf = [sb2(f"ns_bf{i}", [128, 4 * 128], BF16) for i in range(3)]
            A0b12 = sb2("A0b12", [128, 12 * 128], BF16)
            Mb12 = sb2("Mb12", [128, 12 * 128], BF16)
            sqg = [sb2(f"sqg{i}", [128, 128]) for i in range(3)]
            rkcol = sb2("rkcol", [128, 16])
            klcol = sb2("klcol", [128, 32])
            eb_b = [sb2(f"eb_b{i}", [128, 512], BF16) for i in range(2)]
            oi_b = [sb2(f"oi_b{i}", [128, 512], BF16) for i in range(4)]
            ssc = [sb2(f"ssc{i}", [1, 4096], BF16) for i in range(2)]
            scrow = sb2("scrow", [1, 12 * 128])
            p2s = ExitStack()
            sb2b = lambda name, shape, dt=F32: p2s.enter_context(nc.sbuf_tensor(name, shape, dt))
            trbuf = sb2b("trbuf", [128, 2816], BF16)

            def norms_for_chunk(c):
                par = c % 2
                buf = ssc[par]
                rd = [(ss_dram, (r, c)) for r in range(8)]
                p.dma("dsp", buf[0:1, :].rearrange("p (r t) -> p r t", r=8),
                      ss_dram[0:8, c * 512:(c + 1) * 512], reads=rd, writes=[buf])
                p.op("act", lambda e, buf=buf: e.activation(buf[:, :], buf[:, :], AF.Ln),
                     reads=[buf], writes=[buf])
                p.op("act", lambda e, buf=buf: e.activation(buf[:, :], buf[:, :],
                                                            AF.Exp, scale=0.5),
                     reads=[buf], writes=[buf])
                p.op("dve", lambda e, buf=buf: e.tensor_scalar(buf[:, :], buf[:, :],
                                                               1e-5, None, ALU.add),
                     reads=[buf], writes=[buf])
                def _recip(e, buf=buf):
                    with nc.allow_low_precision("bf16 q/k norm factors"):
                        return e.reciprocal(buf[:, :], buf[:, :])
                p.op("dve", _recip, reads=[buf], writes=[buf])

            def prefetch_chunk(c):
                par = c % 2
                for which, buf in ((0, qc_b[par]), (1, kc_b[par]), (2, vc_b[par])):
                    sap = qkv_dram[which * 512:(which + 1) * 512, c * 512:(c + 1) * 512]
                    sap = sap.rearrange("(h d) t -> d h t", d=128)
                    dap = buf[:, :].rearrange("d (h t) -> d h t", h=4)
                    p.dma("dsp", dap, sap,
                          reads=[(qkv_dram, (which * 4 + hh, c)) for hh in range(4)],
                          writes=[buf])
                for tt in range(4):
                    p.dma("dsp", ktvt[par][:, tt * 1024:(tt + 1) * 1024],
                          qkv_dram[4 * 128:12 * 128, c * 512 + tt * 128: c * 512 + (tt + 1) * 128],
                          reads=[(qkv_dram, (j, c)) for j in range(4, 12)],
                          writes=[(ktvt[par], tt)], transpose=True)

            norms_for_chunk(0)
            prefetch_chunk(0)

            for c in range(NCH):
                par = c % 2
                qcb, kcb, vcb, kv = qc_b[par], kc_b[par], vc_b[par], ktvt[par]
                if c + 1 < NCH:
                    prefetch_chunk(c + 1)

                for hh in range(HPC):
                    for tt in range(4):
                        cid = hh * 4 + tt
                        off = (4 + hh) * 512 + tt * 128
                        p.op("pe", lambda e, off=off, cid=cid, par=par: e.matmul(
                            psD[:, cid:cid + 1], ssc[par][0:1, off:off + 128],
                            ones_rb[0:1, 0:1], start=True, stop=True),
                            reads=[ssc[par], ones_rb], writes=[psD])
                p.op("dve", lambda e: e.tensor_copy(rkcol[:, :], psD[:, 0:16]),
                     reads=[psD], writes=[rkcol])
                lrview = lr_sb[:, 48 * c:48 * c + 48].rearrange("p (t J) -> p J t", t=4)
                rkv = rkcol[:, 0:16].rearrange("p (h t) -> p h t", h=4)
                p.op("dve", lambda e, lrview=lrview, rkv=rkv: e.tensor_tensor(
                    klcol[:, 0:16].rearrange("p (h t) -> p h t", h=4),
                    rkv, lrview[:, 0:4, :], ALU.mult),
                    reads=[rkcol, lr_sb], writes=[(klcol, "l0")])
                p.op("dve", lambda e, lrview=lrview, rkv=rkv: e.tensor_tensor(
                    klcol[:, 16:32].rearrange("p (h t) -> p h t", h=4),
                    rkv, lrview[:, 8:12, :], ALU.mult),
                    reads=[rkcol, lr_sb], writes=[(klcol, "l2")])

                for hh in range(HPC):
                    hpar = hh
                    gf, sf, hbf, dhf, sgf, t1f = (gate_f[hh % 2], s_f[hh % 2], hb_f[hh % 2],
                                                  dh_f[hh % 2], sg_f[hh % 2], t1_f[hh % 2])
                    for tt in range(4):
                        ksl = kcb[:, hh * 512 + tt * 128: hh * 512 + (tt + 1) * 128]
                        vsl = vcb[:, hh * 512 + tt * 128: hh * 512 + (tt + 1) * 128]
                        p.op("pe", lambda e, ksl=ksl, tt=tt, hh=hh: e.matmul(
                            psA[:, tt * 128:(tt + 1) * 128], ksl,
                            wbf[:, (hh * 3 + 0) * 128:(hh * 3 + 1) * 128],
                            start=True, stop=True),
                            reads=[kcb, wbf], writes=[psA])
                        p.op("pe", lambda e, ksl=ksl, tt=tt, hh=hh: e.matmul(
                            psB[:, tt * 128:(tt + 1) * 128], ksl,
                            wbf[:, (hh * 3 + 2) * 128:(hh * 3 + 3) * 128],
                            start=True, stop=True),
                            reads=[kcb, wbf], writes=[psB])
                        p.op("pe", lambda e, vsl=vsl, tt=tt, hh=hh: e.matmul(
                            psC[:, tt * 128:(tt + 1) * 128], vsl,
                            w1T[:, hh * 128:(hh + 1) * 128],
                            start=True, stop=True),
                            reads=[vcb, (w1T, hh)], writes=[psC])
                    for tt in range(4):
                        cid = hh * 4 + tt
                        sl = slice(tt * 128, (tt + 1) * 128)
                        p.op("act", lambda e, sl=sl, cid=cid, gf=gf: e.activation(
                            gf[:, sl], psA[:, sl], AF.Copy, scale=rkcol[:, cid:cid + 1]),
                            reads=[psA, rkcol], writes=[gf])
                        p.op("act", lambda e, sl=sl, cid=cid, sf=sf: e.activation(
                            sf[:, sl], psA[:, sl], AF.Sigmoid, scale=rkcol[:, cid:cid + 1]),
                            reads=[psA, rkcol], writes=[sf])
                        p.op("act", lambda e, sl=sl, cid=cid, hbf=hbf: e.activation(
                            hbf[:, sl], psB[:, sl], AF.Copy, scale=rkcol[:, cid:cid + 1]),
                            reads=[psB, rkcol], writes=[hbf])
                    p.op("act", lambda e, dhf=dhf: e.activation(dhf[:, :], psC[:, 0:512], AF.Copy),
                         reads=[psC], writes=[dhf])
                    p.op("dve", lambda e, gf=gf, sf=sf, sgf=sgf: e.tensor_tensor(
                        sgf[:, :], gf[:, :], sf[:, :], ALU.mult),
                        reads=[gf, sf], writes=[sgf])
                    for tt in range(4):
                        l1 = (4 * c + tt) * 12 + 4 + hh
                        slq = slice(tt * 128, (tt + 1) * 128)
                        p.op("dve", lambda e, sgf=sgf, hbf=hbf, slq=slq, l1=l1, hpar=hpar:
                             e.scalar_tensor_tensor(
                                 hidl_b[hpar][:, slq], sgf[:, slq],
                                 lr_sb[:, l1:l1 + 1], hbf[:, slq], ALU.mult, ALU.mult),
                             reads=[sgf, hbf, lr_sb], writes=[(hidl_b[hpar], tt)])
                    p.op("dve", lambda e, dhf=dhf, sgf=sgf, hpar=hpar: e.tensor_tensor(
                        dhbm_b[hpar][:, :], dhf[:, :], sgf[:, :], ALU.mult),
                        reads=[dhf, sgf], writes=[dhbm_b[hpar]])
                    p.op("dve", lambda e, dhf=dhf, hbf=hbf, t1f=t1f: e.tensor_tensor(
                        t1f[:, :], dhf[:, :], hbf[:, :], ALU.mult),
                        reads=[dhf, hbf], writes=[t1f])
                    p.op("dve", lambda e, sgf=sgf, sf=sf, dhf=dhf: e.tensor_tensor(
                        dhf[:, :], sgf[:, :], sf[:, :], ALU.mult),
                        reads=[sgf, sf], writes=[dhf])
                    p.op("dve", lambda e, sgf=sgf, sf=sf: e.tensor_tensor(
                        sgf[:, :], sgf[:, :], sf[:, :], ALU.add),
                        reads=[sgf, sf], writes=[sgf])
                    p.op("dve", lambda e, sgf=sgf, dhf=dhf: e.tensor_tensor(
                        sgf[:, :], sgf[:, :], dhf[:, :], ALU.subtract),
                        reads=[sgf, dhf], writes=[sgf])
                    p.op("dve", lambda e, t1f=t1f, sgf=sgf, hpar=hpar: e.tensor_tensor(
                        dgba_b[hpar][:, :], t1f[:, :], sgf[:, :], ALU.mult),
                        reads=[t1f, sgf], writes=[dgba_b[hpar]])
                    for tt in range(4):
                        cid = hh * 4 + tt
                        ktap = kv[:, tt * 1024 + hh * 128: tt * 1024 + (hh + 1) * 128]
                        p.op("dve", lambda e, ktap=ktap, cid=cid, tt=tt, hpar=hpar: e.tensor_scalar(
                            k0t_b[hpar][:, tt * 128:(tt + 1) * 128], ktap,
                            klcol[:, cid:cid + 1], None, ALU.mult),
                            reads=[(ktvt[par], tt), (klcol, "l0")], writes=[(k0t_b[hpar], tt)])
                        p.op("dve", lambda e, ktap=ktap, cid=cid, tt=tt, hpar=hpar: e.tensor_scalar(
                            k2t_b[hpar][:, tt * 128:(tt + 1) * 128], ktap,
                            klcol[:, 16 + cid:17 + cid], None, ALU.mult),
                            reads=[(ktvt[par], tt), (klcol, "l2")], writes=[(k2t_b[hpar], tt)])

                def grad(pX, lhs, lhsk, rhs, rhsk, gidx, GT, hh):
                    items = []
                    for tt in range(4):
                        lap = lhs(tt) if callable(lhs) else lhs[:, tt * 128:(tt + 1) * 128]
                        rap = rhs(tt) if callable(rhs) else rhs[:, tt * 128:(tt + 1) * 128]
                        items.append((lambda e, pX=pX, lap=lap, rap=rap, tt=tt: e.matmul(
                            pX[:, 0:128], lap, rap, start=(tt == 0), stop=(tt == 3)),
                            lhsk + rhsk, [pX]))
                    p.group("pe", items, reads=lhsk + rhsk, writes=[pX])
                    dst = GTb if GT else Gb
                    p.op("act", lambda e, pX=pX, dst=dst, gidx=gidx: e.activation(
                        dst[:, gidx * 128:(gidx + 1) * 128], pX[:, 0:128], AF.Copy),
                        reads=[pX], writes=[(dst, gidx)])
                    if not GT:
                        sqx = sqg[gidx % 3]
                        p.op("act", lambda e, pX=pX, gidx=gidx, sqx=sqx: e.activation(
                            sqx[:, :], pX[:, 0:128], AF.Square,
                            accum_out=n2acc[:, gidx:gidx + 1]),
                            reads=[pX], writes=[sqx, (n2acc, gidx)])

                gq = 0
                for hh in range(HPC):
                    hl, k0, k2 = hidl_b[hh], k0t_b[hh], k2t_b[hh]
                    dga, dhb = dgba_b[hh], dhbm_b[hh]
                    hlk = [(hl, tt) for tt in range(4)]
                    k0k = [(k0, tt) for tt in range(4)]
                    k2k = [(k2, tt) for tt in range(4)]
                    kvk = [(ktvt[par], tt) for tt in range(4)]

                    def vtile(tt, hh=hh, kv=kv):
                        return kv[:, tt * 1024 + 512 + hh * 128: tt * 1024 + 512 + (hh + 1) * 128]

                    g0 = hh * 3
                    grad(psm[gq % 3], k0, k0k, dga, [dga], g0, False, hh); gq += 1
                    grad(psm[gq % 3], dga, [dga], k0, k0k, g0, True, hh); gq += 1
                    grad(psm[gq % 3], hl, hlk, vtile, kvk, g0 + 1, False, hh); gq += 1
                    grad(psm[gq % 3], vtile, kvk, hl, hlk, g0 + 1, True, hh); gq += 1
                    grad(psm[gq % 3], k2, k2k, dhb, [dhb], g0 + 2, False, hh); gq += 1
                    grad(psm[gq % 3], dhb, [dhb], k2, k2k, g0 + 2, True, hh); gq += 1

                p.op("pe", lambda e: e.matmul(psD[0:1, 0:12], ones_c[:, :], n2acc[:, 0:12],
                                              start=True, stop=True),
                     reads=[ones_c] + [(n2acc, g) for g in range(12)], writes=[psD])
                p.op("dve", lambda e: e.tensor_copy(n2row[0:1, 0:12], psD[0:1, 0:12]),
                     reads=[psD], writes=[n2row])
                p.op("act", lambda e: e.activation(cfrow[0:1, 0:12], n2row[0:1, 0:12], AF.Ln),
                     reads=[n2row], writes=[cfrow])
                p.op("act", lambda e: e.activation(cfrow[0:1, 0:12], cfrow[0:1, 0:12],
                                                   AF.Exp, scale=0.5),
                     reads=[cfrow], writes=[cfrow])
                p.op("dve", lambda e: e.tensor_scalar(cfrow[0:1, 0:12], cfrow[0:1, 0:12],
                                                      1e-7, None, ALU.add),
                     reads=[cfrow], writes=[cfrow])
                p.op("dve", lambda e: e.reciprocal(cfrow[0:1, 0:12], cfrow[0:1, 0:12]),
                     reads=[cfrow], writes=[cfrow])
                p.op("dve", lambda e: e.tensor_tensor(c2row[0:1, 0:12], cfrow[0:1, 0:12],
                                                      cfrow[0:1, 0:12], ALU.mult),
                     reads=[cfrow], writes=[c2row])
                p.op("pe", lambda e: e.matmul(psD[:, 32:44], ones_r[:, :], cfrow[0:1, 0:12],
                                              start=True, stop=True),
                     reads=[ones_r, cfrow], writes=[psD])
                p.op("dve", lambda e: e.tensor_copy(cfcols[:, 0:12], psD[:, 32:44]),
                     reads=[psD], writes=[cfcols])
                p.op("pe", lambda e: e.matmul(psD[:, 48:60], ones_r[:, :], c2row[0:1, 0:12],
                                              start=True, stop=True),
                     reads=[ones_r, c2row], writes=[psD])
                p.op("dve", lambda e: e.tensor_copy(c2cols[:, 0:12], psD[:, 48:60]),
                     reads=[psD], writes=[c2cols])
                if c + 1 < NCH:
                    norms_for_chunk(c + 1)

                def ns_chain(gi):
                    hh, j = gi // 3, gi % 3
                    scrX = ns_scr[gi % 3]
                    bfX = ns_bf[gi % 3]
                    Gap = Gb[:, gi * 128:(gi + 1) * 128]
                    GTap = GTb[:, gi * 128:(gi + 1) * 128]
                    A0b = A0b12[:, gi * 128:(gi + 1) * 128]
                    Mb = Mb12[:, gi * 128:(gi + 1) * 128]
                    Ub, A1b = bfX[:, 0:128], bfX[:, 128:256]
                    M1b, X1b = bfX[:, 256:384], bfX[:, 384:512]
                    cS, mS = scrX[:, 0:128], scrX[:, 128:256]
                    cfc = cfcols[:, gi:gi + 1]
                    c2c = c2cols[:, gi:gi + 1]
                    wap = wf[:, gi * 128:(gi + 1) * 128]
                    wkey = (wf, (hh, j))
                    return [
                        ("mm", lambda pX: (lambda e: e.matmul(pX, GTap, GTap,
                                                              start=True, stop=True),
                                           [(GTb, gi)], [pX])),
                        ("ev", lambda pX: ("act", lambda e: e.activation(
                            A0b, pX, AF.Copy, scale=c2c),
                            [pX, c2cols], [(A0b12, gi)])),
                        ("mm", lambda pX: (lambda e: e.matmul(pX, A0b, A0b,
                                                              start=True, stop=True),
                                           [(A0b12, gi)], [pX])),
                        ("ev", lambda pX: ("dve", lambda e: e.tensor_scalar(
                            cS, pX, NS_C, None, ALU.mult),
                            [pX], [(scrX, 0)])),
                        ("pl", ("dve", lambda e: e.scalar_tensor_tensor(
                            mS, A0b, NS_B, cS, ALU.mult, ALU.add),
                            [(A0b12, gi), (scrX, 0)], [(scrX, 1)])),
                        ("pl", ("dve", lambda e: e.tensor_tensor(Mb, mS, aI[:, :], ALU.add),
                                [(scrX, 1), aI], [(Mb12, gi)])),
                        ("mm", lambda pX: (lambda e: e.matmul(pX, A0b, Mb,
                                                              start=True, stop=True),
                                           [(A0b12, gi), (Mb12, gi)], [pX])),
                        ("ev", lambda pX: ("act", lambda e: e.activation(
                            Ub, pX, AF.Copy), [pX], [(bfX, 0)])),
                        ("mm", lambda pX: (lambda e: e.matmul(pX, Mb, Ub,
                                                              start=True, stop=True),
                                           [(Mb12, gi), (bfX, 0)], [pX])),
                        ("ev", lambda pX: ("dve", lambda e: e.tensor_copy(A1b, pX),
                                           [pX], [(bfX, 1)])),
                        ("mm", lambda pX: (lambda e: e.matmul(pX, A1b, A1b,
                                                              start=True, stop=True),
                                           [(bfX, 1)], [pX])),
                        ("ev", lambda pX: ("act", lambda e: e.activation(
                            cS, pX, AF.Copy, scale=NS_C), [pX], [(scrX, 0)])),
                        ("pl", ("dve", lambda e: e.scalar_tensor_tensor(
                            mS, A1b, NS_B, cS, ALU.mult, ALU.add),
                            [(bfX, 1), (scrX, 0)], [(scrX, 1)])),
                        ("pl", ("dve", lambda e: e.tensor_tensor(M1b, mS, aI[:, :], ALU.add),
                                [(scrX, 1), aI], [(bfX, 2)])),
                        ("mm", lambda pX: (lambda e: e.matmul(pX, Mb, Gap,
                                                              start=True, stop=True),
                                           [(Mb12, gi), (Gb, gi)], [pX])),
                        ("ev", lambda pX: ("act", lambda e: e.activation(
                            X1b, pX, AF.Copy, scale=cfc),
                            [pX, cfcols], [(bfX, 3)])),
                        ("mm", lambda pX: (lambda e: e.matmul(pX, M1b, X1b,
                                                              start=True, stop=True),
                                           [(bfX, 2), (bfX, 3)], [pX])),
                        ("ev", lambda pX: ("dve", lambda e: e.tensor_tensor(
                            wap, wap, pX, ALU.add), [pX, wkey], [wkey])),
                    ]

                chains = [ns_chain(gi) for gi in range(12)]
                sched = []
                for gi in range(12):
                    for s in range(18):
                        sched.append((s + 100 * (gi // 3), gi, s))
                sched.sort(key=lambda x: (x[0], x[1]))
                cur_pX = {}
                mmctr = 0

                def _sub(lst, ap, hnd):
                    return [hnd if x is ap else x for x in lst]

                for _, gi, s in sched:
                    kind, item = chains[gi][s]
                    if kind == "mm":
                        hnd = psm[mmctr % 3]
                        mmctr += 1
                        ap = hnd[:, 0:128]
                        cur_pX[gi] = (hnd, ap)
                        fn, r, wr = item(ap)
                        p.op("pe", fn, reads=_sub(r, ap, hnd), writes=_sub(wr, ap, hnd))
                    elif kind == "ev":
                        hnd, ap = cur_pX[gi]
                        eng, fn, r, wr = item(ap)
                        p.op(eng, fn, reads=_sub(r, ap, hnd), writes=_sub(wr, ap, hnd))
                    else:
                        eng, fn, r, wr = item
                        p.op(eng, fn, reads=r, writes=wr)

                p.op("act", lambda e: e.activation(wsq[:, :], wf[:, :], AF.Square),
                     reads=wfk, writes=[wsq])
                for kb in range(3):
                    p.op("pe", lambda e, k=kb: e.matmul(psD[0:1, 0:512], ones_c[:, :],
                                                        wsq[:, k * 512:(k + 1) * 512],
                                                        start=True, stop=True),
                         reads=[ones_c, wsq], writes=[psD])
                    p.op("dve", lambda e, k=kb: e.tensor_copy(
                        ssw_row[0:1, k * 512:(k + 1) * 512], psD[0:1, 0:512]),
                        reads=[psD], writes=[(ssw_row, kb)])
                p.op("act", lambda e: e.activation(scrow[:, :], ssw_row[:, :], AF.Ln),
                     reads=sswk, writes=[scrow])
                p.op("act", lambda e: e.activation(scrow[:, :], scrow[:, :], AF.Exp, scale=0.5),
                     reads=[scrow], writes=[scrow])
                p.op("dve", lambda e: e.tensor_scalar(scrow[:, :], scrow[:, :], 1e-5, None, ALU.add),
                     reads=[scrow], writes=[scrow])
                p.op("dve", lambda e: e.reciprocal(scrow[:, :], scrow[:, :]),
                     reads=[scrow], writes=[scrow])
                p.op("dve", lambda e: e.tensor_tensor(scrow[:, :], scrow[:, :], wn_row[:, :],
                                                      ALU.mult),
                     reads=[scrow, wn_row], writes=[scrow])
                for kb in range(3):
                    p.op("pe", lambda e, k=kb: e.matmul(
                        psD[:, 0:512], ones_r[:, :], scrow[0:1, k * 512:(k + 1) * 512],
                        start=True, stop=True),
                        reads=[ones_r, scrow], writes=[psD])
                    for q4 in range(4):
                        gi = kb * 4 + q4
                        p.op("dve", lambda e, gi=gi, q4=q4: e.tensor_tensor(
                            wf[:, gi * 128:(gi + 1) * 128], wf[:, gi * 128:(gi + 1) * 128],
                            psD[:, q4 * 128:(q4 + 1) * 128], ALU.mult),
                            reads=[psD, (wf, (gi // 3, gi % 3))],
                            writes=[(wf, (gi // 3, gi % 3))])
                p.op("dve", lambda e: e.tensor_copy(wbf[:, :], wf[:, :]),
                     reads=wfk, writes=[wbf])
                for hh in range(HPC):
                    colw = (hh * 3 + 1) * 128
                    p.op("pe", lambda e, cw=colw: e.transpose(psT[:, :], wbf[:, cw:cw + 128],
                                                              identb[:, :]),
                         reads=[wbf, identb], writes=[psT])
                    p.op("dve", lambda e, hh=hh: e.tensor_copy(
                        w1T[:, hh * 128:(hh + 1) * 128], psT[:, :]),
                        reads=[psT], writes=[(w1T, hh)])

                for hh in range(HPC):
                    hpar = hh % 2
                    p.op("pe", lambda e, hh=hh, par=par: e.matmul(
                        psC[:, 0:512], ones_rb[:, :],
                        ssc[par][0:1, hh * 512:(hh + 1) * 512],
                        start=True, stop=True),
                        reads=[ones_rb, ssc[par]], writes=[psC])
                    p.op("dve", lambda e, hh=hh, qcb=qcb: e.tensor_tensor(
                        qcb[:, hh * 512:(hh + 1) * 512], qcb[:, hh * 512:(hh + 1) * 512],
                        psC[:, 0:512], ALU.mult),
                        reads=[psC, qcb], writes=[qcb])
                    p.op("pe", lambda e, hh=hh, qcb=qcb: e.matmul(
                        psA[:, 0:512], wbf[:, (hh * 3 + 0) * 128:(hh * 3 + 1) * 128],
                        qcb[:, hh * 512:(hh + 1) * 512], start=True, stop=True),
                        reads=[wbf, qcb], writes=[psA])
                    p.op("pe", lambda e, hh=hh, qcb=qcb: e.matmul(
                        psB[:, 0:512], wbf[:, (hh * 3 + 2) * 128:(hh * 3 + 3) * 128],
                        qcb[:, hh * 512:(hh + 1) * 512], start=True, stop=True),
                        reads=[wbf, qcb], writes=[psB])
                    gf, sf = gate_f[hpar], s_f[hpar]
                    p.op("act", lambda e, gf=gf: e.activation(gf[:, :], psA[:, 0:512], AF.Copy),
                         reads=[psA], writes=[gf])
                    p.op("act", lambda e, sf=sf: e.activation(sf[:, :], psA[:, 0:512], AF.Sigmoid),
                         reads=[psA], writes=[sf])
                    p.op("dve", lambda e, gf=gf, sf=sf: e.tensor_tensor(
                        gf[:, :], gf[:, :], sf[:, :], ALU.mult),
                        reads=[gf, sf], writes=[gf])
                    p.op("dve", lambda e, gf=gf, hpar=hpar: e.tensor_tensor(
                        eb_b[hpar][:, :], gf[:, :], psB[:, 0:512], ALU.mult),
                        reads=[gf, psB], writes=[eb_b[hpar]])
                    p.op("pe", lambda e, hh=hh, hpar=hpar: e.matmul(
                        psC[:, 0:512], wbf[:, (hh * 3 + 1) * 128:(hh * 3 + 2) * 128],
                        eb_b[hpar][:, :], start=True, stop=True),
                        reads=[wbf, eb_b[hpar]], writes=[psC])
                    p.op("act", lambda e, hh=hh: e.activation(
                        oi_b[hh][:, :], psC[:, 0:512], AF.Copy),
                        reads=[psC], writes=[oi_b[hh]])
                    p.dma("dact", o_dram[hh * 128:(hh + 1) * 128, c * 512:(c + 1) * 512],
                          oi_b[hh][:, :], reads=[oi_b[hh]],
                          writes=[(o_dram, (hh, c))])

                if c == 5:
                    _p0_transpose(nc, p, wqn_g, wqn_d, wTg_p, wTd_p, wTg, wTd, trbuf)

            p.barrier()
            p2s.close()
            ob = [sb2(f"ob{i}", [128, L], BF16) for i in range(2)]
            osq = sb2("osq", [128, 512])
            ssq3 = sb2("ssq3", [1, L], BF16)
            srb = [sb2(f"srb{i}", [128, 512], BF16) for i in range(2)]
            for hh in range(HPC):
                par = hh % 2
                p.dma("dsp", ob[par][:, :], o_dram[hh * 128:(hh + 1) * 128, :],
                      reads=[(o_dram, (hh, c)) for c in range(NCH)], writes=[ob[par]])
                for s8 in range(8):
                    p.op("act", lambda e, par=par, s8=s8: e.activation(
                        osq[:, :], ob[par][:, s8 * 512:(s8 + 1) * 512], AF.Square),
                        reads=[ob[par]], writes=[osq])
                    p.op("pe", lambda e: e.matmul(psD[0:1, 0:512], ones_c[:, :], osq[:, :],
                                                  start=True, stop=True),
                         reads=[ones_c, osq], writes=[psD])
                    p.op("dve", lambda e, s8=s8: e.tensor_copy(
                        ssq3[0:1, s8 * 512:(s8 + 1) * 512], psD[0:1, 0:512]),
                        reads=[psD], writes=[(ssq3, s8)])
                rk3 = [(ssq3, s8) for s8 in range(8)]
                p.op("dve", lambda e: e.tensor_scalar(
                    ssq3[:, :], ssq3[:, :], 1.0 / 128, 1e-5, ALU.mult, ALU.add),
                    reads=rk3, writes=rk3)
                p.op("act", lambda e: e.activation(ssq3[:, :], ssq3[:, :], AF.Ln),
                    reads=rk3, writes=rk3)
                p.op("act", lambda e: e.activation(
                    ssq3[:, :], ssq3[:, :], AF.Exp, scale=-0.5),
                    reads=rk3, writes=rk3)
                for s8 in range(8):
                    spar = s8 % 2
                    p.op("pe", lambda e, s8=s8: e.matmul(
                        psD[:, 0:512], ones_rb[:, :],
                        ssq3[0:1, s8 * 512:(s8 + 1) * 512],
                        start=True, stop=True),
                        reads=[ones_rb, (ssq3, s8)], writes=[psD])
                    p.op("dve", lambda e, par=par, s8=s8, spar=spar: e.scalar_tensor_tensor(
                        srb[spar][:, :], ob[par][:, s8 * 512:(s8 + 1) * 512],
                        onw_c[:, 0:1], psD[:, 0:512], ALU.mult, ALU.mult),
                        reads=[ob[par], onw_c, psD], writes=[srb[spar]])
                    p.dma("dact",
                          xch_a2a[hh * 128:(hh + 1) * 128, s8 * 512:(s8 + 1) * 512],
                          srb[spar][:, :], reads=[srb[spar]],
                          writes=[(xch_a2a, (hh, s8))])
            p.cc("AllGather", ALU.bypass, G44, [xch_a2a[:, :]], [xch_r[:, :]],
                 reads=[(xch_a2a, (hh, s8)) for hh in range(HPC) for s8 in range(8)],
                 writes=[xch_r])

        p.barrier()

        f4 = ExitStack()
        sb4 = lambda name, shape, dt=F32: f4.enter_context(nc.sbuf_tensor(name, shape, dt))
        with f4:
            nw_r = sb4("nw_r", [1, 512])
            nwB = sb4("nwB", [128, D])
            g_sb = sb4("g_sb", [128, 8 * D], BF16)
            xo_t = [sb4(f"xo_t{i}", [128, D]) for i in range(2)]
            c1cols = sb4("c1cols", [128, 8])
            c2cols2 = sb4("c2cols2", [128, 8])
            am2c = sb4("am2c", [128, 8])
            mgu_c = sb4("mgu_c", [128, 2])
            am_c = sb4("am_c", [128, 12])
            sc_c = sb4("sc_c", [128, 2])
            fm_c = sb4("fm_c", [128, 4])
            fv_c = sb4("fv_c", [128, 4])
            fr_c = sb4("fr_c", [128, 4])
            fr2_c = sb4("fr2_c", [128, 4])
            ft_c = sb4("ft_c", [128, 4])
            sgf2 = [sb4(f"sgf2{i}", [128, 512], BF16) for i in range(2)]

            for nb in range(4):
                p.dma("dsp", nw_r[:, :], nwd[0:1, nb * 512:(nb + 1) * 512],
                      writes=[nw_r])
                p.op("pe", lambda e, nb=nb: e.matmul(
                    psD[:, 0:512], ones_r[:, :], nw_r[0:1, 0:512],
                    start=True, stop=True),
                    reads=[ones_r, nw_r], writes=[psD])
                p.op("dve", lambda e, nb=nb: e.tensor_copy(
                    nwB[:, nb * 512:(nb + 1) * 512], psD[:, 0:512]),
                    reads=[psD], writes=[(nwB, nb)])
            nwk = [(nwB, nb) for nb in range(4)]
            p.op("pe", lambda e: e.matmul(psD[:, 96:98], ones_r[:, :], meanrow[0:1, 0:2],
                                          start=True, stop=True),
                 reads=[ones_r, meanrow], writes=[psD])
            p.op("dve", lambda e: e.tensor_copy(mgu_c[:, :], psD[:, 96:98]),
                 reads=[psD], writes=[mgu_c])

            p.barrier()
            q4s = ExitStack()
            sbq = lambda name, shape, dt=F32: q4s.enter_context(
                nc.sbuf_tensor(name, shape, dt))
            with q4s:
                xnq = [sbq(f"xnq_{i}", [128, D]) for i in range(2)]
                xqb = [sbq(f"xqb_{i}", [128, D], BF16) for i in range(2)]
                xqi = sbq("xqi", [128, D], I32)
                for half in range(2):
                        base = half * 512
                        for tt in range(4):
                            par = tt % 2
                            xt = xo_t[par]
                            grow = base + tt * 128
                            p.dma("dsp", xt[:, :], xqin[grow:grow + 128, :], writes=[xt])
                            p.op("act", lambda e, xt=xt, tt=tt: e.activation(
                                junk[:, :], xt[:, :], AF.Square, accum_out=fv_c[:, tt:tt + 1]),
                                reads=[xt], writes=[junk, (fv_c, tt)])
                            p.op("dve", lambda e, xt=xt, tt=tt: e.reduce_sum(
                                fm_c[:, tt:tt + 1], xt[:, :], AX.X),
                                reads=[xt], writes=[(fm_c, tt)])
                            p.op("dve", lambda e, tt=tt: e.tensor_scalar(
                                fm_c[:, tt:tt + 1], fm_c[:, tt:tt + 1], 1.0 / D, None, ALU.mult),
                                reads=[(fm_c, tt)], writes=[(fm_c, tt)])
                            p.op("dve", lambda e, tt=tt: e.tensor_tensor(
                                fr_c[:, tt:tt + 1], fm_c[:, tt:tt + 1], fm_c[:, tt:tt + 1],
                                ALU.mult),
                                reads=[(fm_c, tt)], writes=[(fr_c, tt)])
                            p.op("dve", lambda e, tt=tt: e.scalar_tensor_tensor(
                                fr_c[:, tt:tt + 1], fv_c[:, tt:tt + 1], 1.0 / D,
                                fr_c[:, tt:tt + 1], ALU.mult, ALU.subtract),
                                reads=[(fv_c, tt), (fr_c, tt)], writes=[(fr_c, tt)])
                        frk = [(fr_c, tt) for tt in range(4)]
                        p.op("dve", lambda e: e.tensor_scalar(fr2_c[:, :], fr_c[:, :],
                                                              1e-5, None, ALU.add),
                             reads=frk, writes=[fr2_c])
                        p.op("act", lambda e: e.activation(fr2_c[:, :], fr2_c[:, :], AF.Ln),
                             reads=[fr2_c], writes=[fr2_c])
                        p.op("act", lambda e: e.activation(fr2_c[:, :], fr2_c[:, :],
                                                           AF.Exp, scale=-0.5),
                             reads=[fr2_c], writes=[fr2_c])
                        p.op("dve", lambda e: e.tensor_tensor(ft_c[:, :], fr2_c[:, :],
                                                              fr2_c[:, :], ALU.mult),
                             reads=[fr2_c], writes=[ft_c])
                        p.op("dve", lambda e: e.tensor_tensor(ft_c[:, :], ft_c[:, :],
                                                              fr_c[:, :], ALU.mult),
                             reads=[ft_c] + frk, writes=[ft_c])
                        p.op("dve", lambda e: e.tensor_scalar(ft_c[:, :], ft_c[:, :],
                                                              1e-5, None, ALU.add),
                             reads=[ft_c], writes=[ft_c])
                        p.op("act", lambda e: e.activation(ft_c[:, :], ft_c[:, :], AF.Ln),
                             reads=[ft_c], writes=[ft_c])
                        p.op("act", lambda e: e.activation(ft_c[:, :], ft_c[:, :],
                                                           AF.Exp, scale=-0.5),
                             reads=[ft_c], writes=[ft_c])
                        p.op("dve", lambda e: e.tensor_tensor(fr2_c[:, :], fr2_c[:, :],
                                                              ft_c[:, :], ALU.mult),
                             reads=[fr2_c, ft_c], writes=[fr2_c])
                        for tt in range(4):
                            par = tt % 2
                            xt, xn, xq = xo_t[par], xnq[par], xqb[par]
                            gtile = half * 4 + tt
                            grow = base + tt * 128
                            p.dma("dsp", xt[:, :], xqin[grow:grow + 128, :], writes=[xt])
                            p.op("dve", lambda e, xt=xt, xn=xn, tt=tt: e.tensor_scalar(
                                xn[:, :], xt[:, :], fm_c[:, tt:tt + 1], fr2_c[:, tt:tt + 1],
                                ALU.subtract, ALU.mult),
                                reads=[xt, (fm_c, tt), fr2_c], writes=[xn])
                            p.op("dve", lambda e, xn=xn: e.tensor_tensor(
                                xn[:, :], xn[:, :], nwB[:, :], ALU.mult),
                                reads=[xn] + nwk, writes=[xn])
                            p.op("dve", lambda e, xn=xn: e.reduce_max(
                                am_c[:, 0:1], xn[:, :], AX.X, apply_absolute_value=True),
                                reads=[xn], writes=[(am_c, 0)])
                            p.op("dve", lambda e: e.tensor_scalar(
                                am_c[:, 0:1], am_c[:, 0:1], 1e-5, None, ALU.max),
                                reads=[(am_c, 0)], writes=[(am_c, 0)])
                            p.op("dve", lambda e, gtile=gtile: e.scalar_tensor_tensor(
                                c1cols[:, gtile:gtile + 1], am_c[:, 0:1], 1.0 / 127,
                                mgu_c[:, 0:1], ALU.mult, ALU.mult),
                                reads=[(am_c, 0), mgu_c], writes=[(c1cols, gtile)])
                            p.op("dve", lambda e: e.reciprocal(sc_c[:, 0:1], am_c[:, 0:1]),
                                 reads=[(am_c, 0)], writes=[sc_c])
                            p.op("dve", lambda e: e.tensor_scalar(
                                sc_c[:, 0:1], sc_c[:, 0:1], 127.0, None, ALU.mult),
                                reads=[sc_c], writes=[sc_c])
                            p.op("dve", lambda e, xn=xn: e.tensor_scalar(
                                xn[:, :], xn[:, :], sc_c[:, 0:1], None, ALU.mult),
                                reads=[xn, sc_c], writes=[xn])
                            p.op("dve", lambda e, xn=xn: e.tensor_copy(xqi[:, :], xn[:, :]),
                                 reads=[xn], writes=[xqi])
                            p.op("dve", lambda e, xq=xq: e.tensor_copy(xq[:, :], xqi[:, :]),
                                 reads=[xqi], writes=[xq])
                            p.dma("dact",
                                  xq_dram[half * 512 + tt * 128: half * 512 + (tt + 1) * 128, :],
                                  xq[:, :], reads=[xq], writes=[(xq_dram, gtile)])
            p.barrier()
            p.op("dve", lambda e: e.memset(am2c[:, :], 0.0),
                 writes=[(am2c, g) for g in range(8)])
            m4s = ExitStack()
            sbm = lambda name, shape, dt=F32: m4s.enter_context(
                nc.sbuf_tensor(name, shape, dt))
            if True:
                        xqfm = [sbm(f"xqfm{i}", [128, 16 * 512], BF16) for i in range(2)]
                        gW = [sbm(f"gW_{i}", [128, 16 * 512], BF16) for i in range(2)]
                        uW = [sbm(f"uW_{i}", [128, 16 * 512], BF16) for i in range(2)]
                        hidst = [sbm(f"hidst_{i}", [128, 512], BF16) for i in range(3)]
                        am_t = sbm("am_t", [128, 3])
                        for half in range(2):
                            for kc in range(16):
                                p.dma("dsp", xqfm[half][:, kc * 512:(kc + 1) * 512],
                                      xq_dram[half * 512:(half + 1) * 512,
                                              kc * 128:(kc + 1) * 128],
                                      reads=[(xq_dram, half * 4 + tt) for tt in range(4)],
                                      writes=[(xqfm[half], kc)], transpose=True)
                        for kp in range(11):
                            wpar = kp % 2
                            for kc in range(16):
                                p.dma("dsp", gW[wpar][:, kc * 512:(kc + 1) * 512],
                                      wTg[kc * 128:(kc + 1) * 128, kp * 512:(kp + 1) * 512],
                                      reads=[wTg], writes=[(gW[wpar], kc)])
                                p.dma("dsp", uW[wpar][:, kc * 512:(kc + 1) * 512],
                                      wTg[kc * 128:(kc + 1) * 128,
                                          FG + kp * 512: FG + (kp + 1) * 512],
                                      reads=[wTg], writes=[(uW[wpar], kc)])
                            for gtile in range(8):
                                half, tt = gtile // 4, gtile % 4
                                xf = xqfm[half]
                                pg = [psA, psC, psG][gtile % 3]
                                pu = [psB, psD, psH][gtile % 3]
                                xk = [(xf, kc) for kc in range(16)]
                                p.group("pe", [
                                    (lambda e, kc=kc, tt=tt, wpar=wpar, pg=pg, xf=xf: e.matmul(
                                        pg[:, 0:512],
                                        xf[:, kc * 512 + tt * 128: kc * 512 + (tt + 1) * 128],
                                        gW[wpar][:, kc * 512:(kc + 1) * 512],
                                        start=(kc == 0), stop=(kc == 15)),
                                     [(xf, kc), (gW[wpar], kc)], [pg])
                                    for kc in range(16)],
                                    reads=xk + [(gW[wpar], kc) for kc in range(16)],
                                    writes=[pg])
                                p.group("pe", [
                                    (lambda e, kc=kc, tt=tt, wpar=wpar, pu=pu, xf=xf: e.matmul(
                                        pu[:, 0:512],
                                        xf[:, kc * 512 + tt * 128: kc * 512 + (tt + 1) * 128],
                                        uW[wpar][:, kc * 512:(kc + 1) * 512],
                                        start=(kc == 0), stop=(kc == 15)),
                                     [(xf, kc), (uW[wpar], kc)], [pu])
                                    for kc in range(16)],
                                    reads=xk + [(uW[wpar], kc) for kc in range(16)],
                                    writes=[pu])
                                spar = gtile % 2
                                p.op("act", lambda e, spar=spar, gtile=gtile, pg=pg: e.activation(
                                    sgf2[spar][:, :], pg[:, 0:512], AF.Relu,
                                    scale=c1cols[:, gtile:gtile + 1]),
                                    reads=[pg, (c1cols, gtile)], writes=[sgf2[spar]])
                                p.op("act", lambda e, spar=spar: e.activation(
                                    sgf2[spar][:, :], sgf2[spar][:, :], AF.Square),
                                    reads=[sgf2[spar]], writes=[sgf2[spar]])
                                hst = hidst[gtile % 3]
                                p.op("dve", lambda e, spar=spar, gtile=gtile, hst=hst, pu=pu:
                                     e.scalar_tensor_tensor(
                                         hst[:, :],
                                         pu[:, 0:512], c1cols[:, gtile:gtile + 1],
                                         sgf2[spar][:, :], ALU.mult, ALU.mult),
                                     reads=[pu, (c1cols, gtile), sgf2[spar]],
                                     writes=[hst])
                                p.op("dve", lambda e, hst=hst, gtile=gtile: e.reduce_max(
                                    am_t[:, gtile % 3:gtile % 3 + 1], hst[:, :], AX.X,
                                    apply_absolute_value=True),
                                    reads=[hst], writes=[(am_t, gtile % 3)])
                                p.op("dve", lambda e, gtile=gtile: e.tensor_tensor(
                                    am2c[:, gtile:gtile + 1], am2c[:, gtile:gtile + 1],
                                    am_t[:, gtile % 3:gtile % 3 + 1], ALU.max),
                                    reads=[(am2c, gtile), (am_t, gtile % 3)],
                                    writes=[(am2c, gtile)])
                                p.dma("dact",
                                      hid_dram[gtile * 128:(gtile + 1) * 128,
                                               kp * 512:(kp + 1) * 512],
                                      hst[:, :], reads=[hst],
                                      writes=[(hid_dram, (half, tt, kp))])
                        m4s.close()
                        p.barrier()
                        q2s = ExitStack()
                        sbq2 = lambda name, shape, dt=F32: q2s.enter_context(
                            nc.sbuf_tensor(name, shape, dt))
                        hload = [sbq2(f"hload_{i}", [128, FG], BF16) for i in range(2)]
                        h2i = sbq2("h2i", [128, 512], I32)
                        for gtile in range(8):
                            half, tt = gtile // 4, gtile % 4
                            hl = hload[gtile % 2]
                            p.dma("dsp", hl[:, :],
                                  hid_dram[gtile * 128:(gtile + 1) * 128, :],
                                  reads=[(hid_dram, (half, tt, kp)) for kp in range(11)],
                                  writes=[hl])
                            p.op("dve", lambda e, gtile=gtile: e.tensor_scalar(
                                am_c[:, 0:1], am2c[:, gtile:gtile + 1], 1e-5, None,
                                ALU.max),
                                reads=[(am2c, gtile)], writes=[(am_c, 0)])
                            p.op("dve", lambda e, gtile=gtile: e.scalar_tensor_tensor(
                                c2cols2[:, gtile:gtile + 1], am_c[:, 0:1], 1.0 / 127,
                                mgu_c[:, 1:2], ALU.mult, ALU.mult),
                                reads=[(am_c, 0), mgu_c], writes=[(c2cols2, gtile)])
                            p.op("dve", lambda e: e.reciprocal(sc_c[:, 0:1], am_c[:, 0:1]),
                                 reads=[(am_c, 0)], writes=[sc_c])
                            p.op("dve", lambda e: e.tensor_scalar(
                                sc_c[:, 0:1], sc_c[:, 0:1], 127.0, None, ALU.mult),
                                reads=[sc_c], writes=[sc_c])
                            for kp in range(11):
                                hap = hl[:, kp * 512:(kp + 1) * 512]
                                p.op("dve", lambda e, hap=hap, kp=kp, hl=hl: e.tensor_scalar(
                                    h2i[:, :], hap, sc_c[:, 0:1], None, ALU.mult),
                                    reads=[(hl, kp), sc_c], writes=[h2i])
                                p.op("dve", lambda e, hap=hap, kp=kp, hl=hl: e.tensor_copy(
                                    hap, h2i[:, :]),
                                    reads=[h2i], writes=[(hl, kp)])
                            p.dma("dact",
                                  xq2_dram[gtile * 128:(gtile + 1) * 128, :],
                                  hl[:, :], reads=[(hl, kp) for kp in range(11)],
                                  writes=[(xq2_dram, gtile)])
                        q2s.close()
            p.barrier()
            h5 = ExitStack()
            sbh5 = lambda name, shape, dt=F32: h5.enter_context(
                nc.sbuf_tensor(name, shape, dt))
            with h5:
                xq2fm = sbh5("xq2fm", [128, 44 * 512], BF16)
                wdW = [sbh5(f"wdW_{i}", [128, 11 * 512], BF16) for i in range(2)]
                for half in range(2):
                    for kc in range(44):
                        p.dma("dsp", xq2fm[:, kc * 512:(kc + 1) * 512],
                              xq2_dram[half * 512:(half + 1) * 512,
                                       kc * 128:(kc + 1) * 128],
                              reads=[(xq2_dram, half * 4 + tt) for tt in range(4)],
                              writes=[(xq2fm, (half, kc))], transpose=True)
                    for dt4 in range(4):
                        for st4 in range(4):
                            wb = wdW[st4 % 2]
                            for kk in range(11):
                                kc = st4 * 11 + kk
                                p.dma("dsp", wb[:, kk * 512:(kk + 1) * 512],
                                      wTd[kc * 128:(kc + 1) * 128,
                                          dt4 * 512:(dt4 + 1) * 512],
                                      reads=[wTd], writes=[(wb, kk)])
                            for tt in range(4):
                                pX = [psA, psB, psC, psD][tt]
                                p.group("pe", [
                                    (lambda e, pX=pX, kc=st4 * 11 + kk, kk=kk, tt=tt, wb=wb:
                                     e.matmul(
                                         pX[:, 0:512],
                                         xq2fm[:, kc * 512 + tt * 128:
                                               kc * 512 + (tt + 1) * 128],
                                         wb[:, kk * 512:(kk + 1) * 512],
                                         start=(kc == 0), stop=(kc == 43)),
                                     [(xq2fm, (half, st4 * 11 + kk)), (wb, kk)], [pX])
                                    for kk in range(11)],
                                    reads=[(xq2fm, (half, st4 * 11 + kk))
                                           for kk in range(11)] +
                                          [(wb, kk) for kk in range(11)],
                                    writes=[pX])
                        for tt in range(4):
                            pX = [psA, psB, psC, psD][tt]
                            gtile = half * 4 + tt
                            spar = tt % 2
                            p.op("act", lambda e, pX=pX, spar=spar, gtile=gtile: e.activation(
                                sgf2[spar][:, :], pX[:, 0:512], AF.Sigmoid,
                                scale=c2cols2[:, gtile:gtile + 1]),
                                reads=[pX, (c2cols2, gtile)], writes=[sgf2[spar]])
                            p.op("dve", lambda e, pX=pX, spar=spar, gtile=gtile, dt4=dt4:
                                 e.scalar_tensor_tensor(
                                     g_sb[:, gtile * D + dt4 * 512: gtile * D + (dt4 + 1) * 512],
                                     pX[:, 0:512], c2cols2[:, gtile:gtile + 1],
                                     sgf2[spar][:, :], ALU.mult, ALU.mult),
                                 reads=[pX, (c2cols2, gtile), sgf2[spar]],
                                 writes=[(g_sb, (gtile, dt4))])

            p.barrier()
            f5 = ExitStack()
            sb5 = lambda name, shape, dt=F32: f5.enter_context(nc.sbuf_tensor(name, shape, dt))
            with f5:
                srm = sb5("srm", [128, 16 * 1024], BF16)
                wp_sb = [sb5(f"wp_sb{i}", [128, 16 * 512], BF16) for i in range(2)]
                out_sb = [sb5(f"out_sb{i}", [128, 512]) for i in range(2)]
                _qoff = []

                def _srm_load(e, kc):
                    if not _qoff:
                        _qoff.append((nc.partition_id() % 4) * 1024)
                    return e.dma_start(srm[:, kc * 1024:(kc + 1) * 1024],
                                       xch_r[kc * 128:(kc + 1) * 128,
                                             bass.ds(_qoff[0], 1024)])

                for kc in range(16):
                    p.dma_fn("dpool", (lambda e, kc=kc: _srm_load(e, kc)),
                             reads=[xch_r], writes=[(srm, kc)])
                for nh in range(4):
                    wp = wp_sb[nh % 2]
                    for kc in range(16):
                        p.dma("dpool", wp[:, kc * 512:(kc + 1) * 512],
                              wprojd[kc * 128:(kc + 1) * 128, nh * 512:(nh + 1) * 512],
                              reads=[], writes=[(wp, kc)])
                    for tt in range(8):
                        par = tt % 2
                        xt, ob_ = xo_t[par], out_sb[par]
                        ncol = nh * 512
                        p.dma("dsp", xt[:, 0:512],
                              xqin[tt * 128:(tt + 1) * 128, ncol:ncol + 512],
                              writes=[xt])
                        pX = [psA, psB][tt % 2]
                        p.group("pe", [
                            (lambda e, pX=pX, kc=kc, tt=tt, wp=wp: e.matmul(
                                pX[:, 0:512],
                                srm[:, kc * 1024 + tt * 128: kc * 1024 + (tt + 1) * 128],
                                wp[:, kc * 512:(kc + 1) * 512],
                                start=(kc == 0), stop=(kc == 15)),
                             [(srm, kc), (wp, kc)], [pX])
                            for kc in range(16)],
                            reads=[(srm, kc) for kc in range(16)] +
                                  [(wp, kc) for kc in range(16)],
                            writes=[pX])
                        p.op("dve", lambda e, pX=pX, tt=tt, ncol=ncol, ob_=ob_:
                             e.tensor_tensor(
                                 ob_[:, 0:512], pX[:, 0:512],
                                 g_sb[:, tt * D + ncol: tt * D + ncol + 512], ALU.mult),
                             reads=[pX, (g_sb, (tt, ncol // 512))], writes=[ob_])
                        p.op("dve", lambda e, ob_=ob_, xt=xt:
                             e.tensor_tensor(
                                 ob_[:, 0:512], ob_[:, 0:512],
                                 xt[:, 0:512], ALU.add),
                             reads=[ob_, xt], writes=[ob_])
                        p.dma("dpool", OUT[tt * 128:(tt + 1) * 128, nh * 512:(nh + 1) * 512],
                              ob_[:, 0:512], reads=[ob_], writes=[(OUT, (tt, nh))])

        for dn, dst, src in dump_outs:
            p.dma("dpool", dst[:, :], src[:, :], reads=[src], writes=[dst])
        p.finish_wait("sp")
        p.emit()

    return nc


def _p0_tiles(wgud, wdnd):
    gu_t = [wgud[i * 512:(i + 1) * 512, :].rearrange("(r p) c -> p r c", p=128)
            for i in range(22)]
    dn_t = [wdnd[i * 128:(i + 1) * 128, :] for i in range(16)]
    return ([("g", i, gu_t[i], 1024) for i in range(22)] +
            [("d", i, dn_t[i], 768) for i in range(16)])


def _p0_sbap(buf, kind, w):
    if kind == "g":
        return buf[:, 0:w].rearrange("p (r c) -> p r c", c=256)
    return buf[:, 0:w]


def _p0_abs_part(nc, p, tiles, lo, hi, p0a, racc, junk):
    for idx in range(lo, hi):
        kind, i, ap, w = tiles[idx]
        par = idx % 2
        p.dma("dpool", _p0_sbap(p0a[par], kind, w), ap, writes=[p0a[par]])
        p.op("act", lambda e, par=par, w=w, idx=idx: e.activation(
            junk[:, 0:w], p0a[par][:, 0:w], AF.Abs, accum_out=racc[:, idx:idx + 1]),
            reads=[p0a[par]], writes=[junk, (racc, idx)])


def _p0_reduce_a(nc, p, red_in, red_out, red_sb, red_sb2, invn_sb, meanrow,
                 thrrow, racc, ones_c, psD):
    p.op("pe", lambda e: e.matmul(psD[0:1, 64:104], ones_c[:, :], racc[:, 0:40],
                                  start=True, stop=True),
         reads=[ones_c] + [(racc, i) for i in range(38)], writes=[psD])
    p.op("dve", lambda e: e.reduce_sum(red_sb[0:1, 0:1], psD[0:1, 64:86], AX.X),
         reads=[psD], writes=[(red_sb, 0)])
    p.op("dve", lambda e: e.reduce_sum(red_sb[0:1, 1:2], psD[0:1, 86:102], AX.X),
         reads=[psD], writes=[(red_sb, 1)])
    p.op("dve", lambda e: e.memset(red_sb[0:1, 2:4], 0.0), writes=[(red_sb, 2)])
    p.dma("dpool", red_in[:, :], red_sb[:, :],
          reads=[(red_sb, 0), (red_sb, 1), (red_sb, 2)], writes=[red_in])
    p.cc("AllReduce", ALU.add, G8, [red_in[:, :]], [red_out[:, :]],
         reads=[red_in], writes=[red_out])
    p.dma("dpool", red_sb2[:, :], red_out[:, :], reads=[red_out], writes=[red_sb2])
    p.op("pool", lambda e: e.tensor_tensor(meanrow[:, :], red_sb2[:, :], invn_sb[:, :],
                                           ALU.mult),
         reads=[red_sb2, invn_sb], writes=[meanrow])
    p.op("pool", lambda e: e.tensor_scalar(meanrow[:, :], meanrow[:, :], 1e-5, None, ALU.max),
         reads=[meanrow], writes=[meanrow])
    p.op("pool", lambda e: e.tensor_scalar(thrrow[:, :], meanrow[:, :], 0.5, None, ALU.mult),
         reads=[meanrow], writes=[thrrow])


def _p0_reduce_b(nc, p, thrrow, thr4, nthr4, ones_r, psD):
    p.op("pe", lambda e: e.matmul(psD[:, 96:100], ones_r[:, :], thrrow[0:1, 0:4],
                                  start=True, stop=True),
         reads=[ones_r, thrrow], writes=[psD])
    p.op("dve", lambda e: e.tensor_copy(thr4[:, :], psD[:, 96:100]),
         reads=[psD], writes=[thr4])
    p.op("pool", lambda e: e.tensor_scalar(nthr4[:, :], thr4[:, :], -1.0, None, ALU.mult),
         reads=[thr4], writes=[nthr4])


def _p0_quant_part(nc, p, tiles, lo, hi, p0a, qjb, thr4, nthr4, wqn_g, wqn_d):
    for idx in range(lo, hi):
        kind, i, ap, w = tiles[idx]
        par = idx % 2
        tc = 0 if kind == "g" else 1
        p.dma("dpool", _p0_sbap(p0a[par], kind, w), ap, writes=[p0a[par]])
        for half in range(2):
            c0 = half * 512
            c1 = min(w, c0 + 512)
            sw = c1 - c0
            p.op("dve", lambda e, par=par, c0=c0, c1=c1, sw=sw, tc=tc: e.tensor_scalar(
                qjb[:, 0:sw], p0a[par][:, c0:c1], nthr4[:, tc:tc + 1], None, ALU.is_lt),
                reads=[p0a[par], nthr4], writes=[qjb])
            p.op("dve", lambda e, par=par, c0=c0, c1=c1, sw=sw, tc=tc:
                 e.scalar_tensor_tensor(
                     qjb[:, 0:sw], p0a[par][:, c0:c1], thr4[:, tc:tc + 1],
                     qjb[:, 0:sw], ALU.is_gt, ALU.subtract),
                 reads=[p0a[par], thr4, qjb], writes=[qjb])
            if kind == "g":
                dst = wqn_g[i * 512:(i + 1) * 512, :].rearrange(
                    "(r p) c -> p r c", p=128)[:, half * 2:half * 2 + 2, :]
                p.dma("dpool", dst, qjb[:, 0:512].rearrange("p (r c) -> p r c", c=256),
                      reads=[qjb], writes=[(wqn_g, (i, half))])
            else:
                p.dma("dpool", wqn_d[i * 128:(i + 1) * 128, c0:c1], qjb[:, 0:sw],
                      reads=[qjb], writes=[(wqn_d, (i, half))])


def _p0_transpose(nc, p, wqn_g, wqn_d, wTg_p, wTd_p, wTg, wTd, trbuf):
    gk = [(wqn_g, (i, h)) for i in range(22) for h in range(2)]
    dk = [(wqn_d, (i, h)) for i in range(16) for h in range(2)]
    for cc in range(2):
        for hf in range(4):
            p.dma("dsp", trbuf[:, 0:2816],
                  wqn_g[hf * 2816:(hf + 1) * 2816, cc * 128:(cc + 1) * 128],
                  reads=gk, writes=[trbuf], transpose=True)
            p.dma("dpool", wTg_p[cc * 128:(cc + 1) * 128, hf * 2816:(hf + 1) * 2816],
                  trbuf[:, 0:2816], reads=[trbuf], writes=[(wTg_p, (cc, hf))])
    for cc in range(6):
        p.dma("dsp", trbuf[:, 0:D], wqn_d[:, cc * 128:(cc + 1) * 128],
              reads=dk, writes=[trbuf], transpose=True)
        p.dma("dpool", wTd_p[cc * 128:(cc + 1) * 128, :], trbuf[:, 0:D],
              reads=[trbuf], writes=[(wTd_p, cc)])
    p.cc("AllGather", ALU.bypass, G8, [wTg_p[:, :]], [wTg[:, :]],
         reads=[(wTg_p, (cc, hf)) for cc in range(2) for hf in range(4)], writes=[wTg])
    p.cc("AllGather", ALU.bypass, G8, [wTd_p[:, :]], [wTd[:, :]],
         reads=[(wTd_p, cc) for cc in range(6)], writes=[wTd])


_NC_CACHE = []


def kernel(L0_attn_x, L0_attn_w0, L0_attn_w1, L0_attn_w2, L0_attn_W_qkv,
           L0_attn_W_lr, L0_attn_b_lr, L0_attn_W_proj, L0_attn_onorm_w,
           L0_ffn_w_gate_up, L0_ffn_w_down, L0_ffn_norm_weight):
    x = np.ascontiguousarray(np.asarray(L0_attn_x), dtype=np.float32)
    Wqkv = np.asarray(L0_attn_W_qkv, dtype=np.float32)
    Wlr = np.asarray(L0_attn_W_lr, dtype=np.float32)
    b_lr = np.asarray(L0_attn_b_lr, dtype=np.float32)
    w0 = np.asarray(L0_attn_w0, dtype=np.float32)
    w1 = np.asarray(L0_attn_w1, dtype=np.float32)
    w2 = np.asarray(L0_attn_w2, dtype=np.float32)
    onw = np.asarray(L0_attn_onorm_w, dtype=np.float32)
    Wproj = np.ascontiguousarray(np.asarray(L0_attn_W_proj, dtype=np.float32))
    wgu = np.asarray(L0_ffn_w_gate_up, dtype=np.float32)
    wdn = np.asarray(L0_ffn_w_down, dtype=np.float32)
    nw = np.asarray(L0_ffn_norm_weight, dtype=np.float32)

    in_maps = []
    for core in range(8):
        b, g = core // 4, core % 4
        heads = [4 * g + hh for hh in range(HPC)]
        qkv_cols = [j * D + h * HD + d for j in range(3) for h in heads for d in range(HD)]
        lr_cols = [j * 16 + h for j in range(3) for h in heads]
        wdn_s = np.zeros((D, DN_PAD), np.float32)
        lo = core * DN_PAD
        hi = min(lo + DN_PAD, FG)
        if hi > lo:
            wdn_s[:, 0:hi - lo] = wdn[:, lo:hi]
        in_maps.append({
            "x_b": np.ascontiguousarray(x[b]),
            "x_q": np.ascontiguousarray(x[b][g * 1024:(g + 1) * 1024]),
            "wqkv": np.ascontiguousarray(Wqkv[:, qkv_cols]),
            "wlr": np.ascontiguousarray(Wlr[:, lr_cols]),
            "blr": np.ascontiguousarray(np.tile(b_lr[lr_cols][None, :], (1, 4))),
            "w0": np.ascontiguousarray(w0[heads].reshape(HPC * HD, HD)),
            "w1": np.ascontiguousarray(w1[heads].reshape(HPC * HD, HD)),
            "w2": np.ascontiguousarray(w2[heads].reshape(HPC * HD, HD)),
            "onw": np.ascontiguousarray(onw[None, :]),
            "wproj": Wproj,
            "wgu": np.ascontiguousarray(wgu[:, core * 256:(core + 1) * 256]),
            "wdn": wdn_s,
            "nw": np.ascontiguousarray(nw[None, :]),
        })

    if not _NC_CACHE:
        _NC_CACHE.append(build_program())
    nc = _NC_CACHE[0]
    res = run_bass_kernel_spmd(nc, in_maps, core_ids=list(range(8)))
    out = np.empty((2, L, D), np.float32)
    for core in range(8):
        b, g = core // 4, core % 4
        out[b, g * 1024:(g + 1) * 1024, :] = res.results[core]["out"]
    return out
